# version 21
# baseline (speedup 1.0000x reference)
import numpy as np
import concourse.bass as bass
import concourse.mybir as mybir
from concourse.bass_utils import run_bass_kernel_spmd
from contextlib import ExitStack

F32 = mybir.dt.float32
BF16 = mybir.dt.bfloat16
AX = mybir.AxisListType.X
OP = mybir.AluOpType
AF = mybir.ActivationFunctionType

NCORES = 8
NT = 4096
D = 1024
H = 4096
NG = 32
NTILE = 16
EPS = 1e-5
QEPS = 1e-6
INV_D = 1.0 / 1024.0
INV_H = 1.0 / 4096.0




_DRAIN_OPS = frozenset((
    "tensor_copy", "tensor_tensor", "tensor_add", "tensor_sub", "tensor_mul",
    "tensor_max", "tensor_scalar", "tensor_scalar_mul", "tensor_scalar_add",
    "tensor_scalar_sub", "tensor_single_scalar", "scalar_tensor_tensor",
    "reduce_sum", "reduce_max", "tensor_reduce", "reciprocal", "memset",
    "activation", "bn_stats", "bn_aggr", "select", "square", "sqrt", "copy",
    "add", "mul", "nop",
))


class _AutoDrain:

    def __init__(self, eng):
        self._eng = eng
        self._last = None

    def inc(self, sem, n):
        self._last.then_inc(sem, n)

    def __getattr__(self, name):
        attr = getattr(self._eng, name)
        if name in _DRAIN_OPS:
            def wrapped(*a, **kw):
                self._eng.drain()
                ins = attr(*a, **kw)
                self._last = ins
                return ins
            return wrapped
        return attr


def _split3(eng, src_ap, dst_hi, dst_mid, dst_lo, rr1, rr2=None):
    eng.tensor_copy(out=dst_hi, in_=src_ap)
    eng.tensor_tensor(out=rr1, in0=src_ap, in1=dst_hi, op=OP.subtract)
    eng.tensor_copy(out=dst_mid, in_=rr1)
    eng.tensor_tensor(out=dst_lo, in0=rr1, in1=dst_mid, op=OP.subtract)


def build_nc(nt=NTILE, do_b=True, do_main=True):
    ng = 2 * nt
    nc = bass.Bass("TRN2", target_bir_lowering=False, debug=False)

    dx = nc.dram_tensor("x", [NT, D], F32, kind="ExternalInput").ap()
    dwvT = nc.dram_tensor("wvT", [D, D], F32, kind="ExternalInput").ap()
    dwoT = nc.dram_tensor("woT", [D, D], F32, kind="ExternalInput").ap()
    dwfcT = nc.dram_tensor("wfcT", [D, H], F32, kind="ExternalInput").ap()
    dwpT = nc.dram_tensor("wpT", [H, D], F32, kind="ExternalInput").ap()
    dlnw = nc.dram_tensor("lnw", [128, 32], F32, kind="ExternalInput").ap()
    dlnb = nc.dram_tensor("lnb", [128, 32], F32, kind="ExternalInput").ap()
    dident = nc.dram_tensor("ident", [128, 128], F32, kind="ExternalInput").ap()
    dout = nc.dram_tensor("out", [NT, D], F32, kind="ExternalOutput").ap()

    es = ExitStack()
    sb = lambda n, shp, dt=F32: es.enter_context(nc.sbuf_tensor(n, list(shp), dt))
    ps = lambda n, shp: es.enter_context(nc.psum_tensor(n, list(shp), F32))
    sem = lambda n: es.enter_context(nc.semaphore(n))

    with es:
        qf_t = sb("qf_t", (128, 8 * H), BF16)
        qf = [qf_t[:, d * H:(d + 1) * H] for d in range(8)]
        qp_t = sb("qp_t", (128, 32 * D), BF16)
        qp = [qp_t[:, j * D:(j + 1) * D] for j in range(32)]
        lnw_c = sb("lnw_c", (128, 32))
        lnb_c = sb("lnb_c", (128, 32))
        lnwb = [sb(f"lnwb{s}", (128, 32), BF16) for s in range(3)]
        lnbb = [sb(f"lnbb{s}", (128, 32), BF16) for s in range(3)]
        ones_col = sb("ones_col", (128, 1), BF16)
        ones_row = sb("ones_row", (1, 128), BF16)
        identb = sb("identb", (128, 128), BF16)
        U = sb("U", (128, D))
        V = sb("V", (128, D))
        ATT0 = sb("ATT0", (128, D))
        ATT1 = sb("ATT1", (128, D))
        ATT = [ATT0, ATT1]

        xs = [sb(f"xs{k}", (128, D)) for k in range(2)]
        x1s = [sb(f"x1s{k}", (128, D)) for k in range(4)]
        l2t_t = [sb(f"l2t_t{p}", (128, 3 * 8 * 256), BF16) for p in range(2)]
        l2t = [[[l2t_t[p][:, (s * 8 + d) * 256:(s * 8 + d) * 256 + 256] for d in range(8)]
                for s in range(3)] for p in range(2)]
        l2w = [[l2t_t[p][:, s * 2048:s * 2048 + 1024] for s in range(3)]
               for p in range(2)]
        g = [sb("g0", (128, 256))] * 2
        gsq = [sb("gsq0", (128, 256))] * 2
        gp = g
        gps_t = sb("gps_t", (128, 2304), BF16)
        gps = [[gps_t[:, (k * 3 + s) * 256:(k * 3 + s) * 256 + 256] for s in range(3)]
               for k in range(3)]
        rrq1 = sb("rrq1", (128, 256))
        rrq2 = sb("rrq2", (128, 256))
        rr1 = rrq1[:, 0:128]
        rr2 = rrq1[:, 128:256]
        S1 = sb("S1", (128, 256))
        S2 = sb("S2", (128, 256))
        Ss_t = sb("Ss_t", (128, 1536), BF16)
        Ss = [Ss_t[:, 256 * k:256 * k + 256] for k in range(6)]

        bnp = sb("bnp", (128, 12))
        bno = sb("bno", (128, 2))
        bno2 = sb("bno2", (128, 4))
        cols = sb("cols", (128, 32))
        _ci = iter(range(32))
        col = lambda: cols[:, (lambda k: slice(k, k + 1))(next(_ci))]
        c1c = col(); c2c = col()
        m2c = [col() for k in range(2)]
        t2c = col(); varc = col()
        rcc = [col() for k in range(2)]
        r2c = [col() for k in range(2)]
        biasc = [col() for k in range(2)]
        m3c = [col() for k in range(2)]
        q3c = col(); t3c = col(); var3c = col()
        rc3c = [col() for k in range(2)]
        r3c = [col() for k in range(2)]
        nmrc = [col() for k in range(2)]
        phalf = col(); mhalf = col(); quarc = col()
        sum3 = rrq2[:, 0:128]

        svc = sb("svc", (128, 16))
        svb = [sb(f"svb{s}", (128, 16), BF16) for s in range(3)]
        vcols = svc
        vb = [sb(f"vb{s}", (128, 16), BF16) for s in range(3)]
        acols = svc
        ab = svb
        rowb = [l2t_t[0][0:1, 0:D], l2t_t[0][0:1, D:2 * D]]
        raccf = col(); raccp = col()
        rtmp = col()
        raccb = [svb[s][:, 0:1] for s in range(3)]
        gcol = sb("gcol", (128, 8), BF16)
        rowsT = gps_t
        raccbp = [gcol[:, s:s + 1] for s in range(3)]
        gambp = [gcol[0:1, 4 + s:5 + s] for s in range(3)]
        gamr = rrq2[0:1, 226:227]; gamr2 = rrq2[0:1, 227:228]
        gamb = [svb[s][0:1, 1:2] for s in range(3)]
        rcg = [col() for k in range(2)]
        uvc = svc
        uvb = vb
        rr1s = rrq2[:, 128:144]; rr2s = rrq2[:, 144:160]
        rr1w = rrq2[:, 160:192]; rr2w = rrq2[:, 192:224]
        rr1g = rrq2[0:1, 224:225]; rr2g = rrq2[0:1, 225:226]

        fc_ps = [ps("fc_ps0", (128, 256)), ps("fc_ps1", (128, 256))]
        proj_ps = [ps("proj_ps0", (128, D)), ps("proj_ps1", (128, D))]
        trp_ps = [ps("trp_ps0", (128, 128)), ps("trp_ps1", (128, 128))]

        A_cfg = sem("A_cfg")
        A_xs = [sem("A_xs0"), sem("A_xs1")]
        A_wvs = [sem("A_wvs0"), sem("A_wvs1")]
        A_wos = [sem("A_wos0"), sem("A_wos1")]
        st_trA = sem("st_trA"); ss_trevA = sem("ss_trevA")
        A_rows = [sem("A_rows0"), sem("A_rows1")]
        A_sq = sem("A_sq"); A_rc = sem("A_rc"); A_r1 = sem("A_r1"); A_bias = sem("A_bias")
        A_ln1s = sem("A_ln1s"); A_ln1 = sem("A_ln1"); A_accsplit = sem("A_accsplit")
        A_accred = sem("A_accred"); A_id = sem("A_id")
        A_sv = sem("A_sv"); A_wvsplit = sem("A_wvsplit"); A_wosplit = sem("A_wosplit")
        st_wv = sem("st_wv"); st_wo = sem("st_wo"); A_v = sem("A_v")
        A_attnsplit = sem("A_attnsplit"); st_attS = sem("st_attS")
        A_attevac = sem("A_attevac")
        B_x1s = [sem("B_x1s0"), sem("B_x1s1")]
        B_x2s = [sem("B_x2s0"), sem("B_x2s1")]
        B_scan = sem("B_scan")
        B_x1p = [sem("B_x1p0"), sem("B_x1p1")]
        B_x2p = [sem("B_x2p0"), sem("B_x2p1")]
        B_scanp = sem("B_scanp"); B_qp = sem("B_qp")
        B_sgf = sem("B_sgf"); B_sgp = sem("B_sgp")
        st_abT = sem("st_abT"); ss_abT = sem("ss_abT")
        sv_psr = sem("sv_psr"); st_p31a = sem("st_p31a")
        B_g1 = sem("B_g1"); st_g1 = sem("st_g1"); B_g2 = sem("B_g2")
        st_g2 = sem("st_g2"); B_g3 = sem("B_g3"); B_q = sem("B_q")
        B_lnw = sem("B_lnw"); st_uv = sem("st_uv"); B_uv = sem("B_uv")
        B_rows = [sem("B_rows0"), sem("B_rows1")]; st_ubcS = sem("st_ubcS")
        B_uvbc = sem("B_uvbc")
        sm_xs = [sem("sm_xs0"), sem("sm_xs1")]
        st_tr = sem("st_tr"); ss_trev = sem("ss_trev")
        st_trS = sem("st_trS"); ss_trS = sem("ss_trS")
        sm_outs = [sem(f"sm_outs{k}") for k in range(4)]
        sv_x1 = sem("sv_x1"); sv_stat = sem("sv_stat"); sv_bias = sem("sv_bias")
        sv_stg = sem("sv_stg"); sv_gdone = sem("sv_gdone"); sv_yraw = sem("sv_yraw")
        sv_Ssplit = sem("sv_Ssplit"); sv_rc3 = sem("sv_rc3"); sv_y3 = sem("sv_y3")
        ss_sq = sem("ss_sq"); ss_r2 = sem("ss_r2"); ss_ln2 = sem("ss_ln2")
        ss_r3 = sem("ss_r3"); ss_gelu = sem("ss_gelu"); ss_gsq = sem("ss_gsq")
        st_fcj = sem("st_fcj"); st_projj = sem("st_projj")

        xg = dx.rearrange("(g p) d -> g p d", p=128)
        og = dout.rearrange("(g p) d -> g p d", p=128)
        wfcTg = dwfcT.rearrange("(t p) h -> t p h", p=128)
        wpTg = dwpT.rearrange("(t p) d -> t p d", p=128)
        wvTg = dwvT.rearrange("(t p) e -> t p e", p=128)
        woTg = dwoT.rearrange("(t p) o -> t p o", p=128)

        def emit_proj(tensor, i, jj):
            cglob = 32 * i + jj
            tensor.wait_ge(sv_gdone, cglob + 1)
            if jj == 0 and i == 0:
                tensor.wait_ge(B_qp, 32)
                tensor.wait_ge(B_uvbc, 2)
            last = None
            for tg in range(2):
                if jj == 0 and i >= 1:
                    tensor.wait_ge(sv_psr, 2 * i - 1 + tg)
                for s in range(3):
                    for hh in range(2):
                        last = tensor.matmul(out=proj_ps[tg][:, 512 * hh:512 * hh + 512],
                                      lhsT=gps[cglob % 3][s][:, 128 * tg:128 * tg + 128],
                                      rhs=qp[jj][:, 512 * hh:512 * hh + 512],
                                      start=(jj == 0 and s == 0),
                                      stop=(jj == 31 and s == 2),
                                      skip_group_check=True)
                if jj == 31 and tg == 0:
                    last.then_inc(st_p31a, 1)
            last.then_inc(st_projj, 1)

        def emit_epilogue(vector, ii):
            vector.wait_ge(ss_r3, ii + 1)
            for tg in range(2):
                vector.tensor_scalar(out=nmrc[tg][:], in0=m3c[tg][:], scalar1=r3c[tg][:],
                                     scalar2=-2.0, op0=OP.mult, op1=OP.mult)
            for tg in range(2):
                gr = 2 * ii + tg
                if tg == 0:
                    vector.wait_ge(st_p31a, ii + 1)
                else:
                    vector.wait_ge(st_projj, 32 * (ii + 1))
                vector.scalar_tensor_tensor(out=x1s[gr % 4][:], in0=proj_ps[tg][:],
                                            scalar=r3c[tg][:], in1=x1s[gr % 4][:],
                                            op0=OP.mult, op1=OP.add)
                vector.inc(sv_psr, 1)
            for tg in range(2):
                gr = 2 * ii + tg
                vector.scalar_tensor_tensor(out=x1s[gr % 4][:], in0=U[:],
                                            scalar=nmrc[tg][:], in1=x1s[gr % 4][:],
                                            op0=OP.mult, op1=OP.add)
                vector.tensor_add(out=x1s[gr % 4][:], in0=x1s[gr % 4][:], in1=V[:])
                vector.inc(sv_y3, 1)

        def emit_x1_stats(vector, ip1, only_tg=None):
            for tg in ((0, 1) if only_tg is None else (only_tg,)):
                gr = 2 * ip1 + tg
                vector.wait_ge(sm_xs[gr % 2], 16 * (gr // 2 + 1))
                if gr >= 4:
                    vector.wait_ge(sm_outs[gr % 4], 16 * (gr // 4))
                vector.tensor_add(out=x1s[gr % 4][:], in0=xs[gr % 2][:],
                                  in1=ATT[gr // 16][:])
                vector.bn_stats(out=bnp[:, 0:6], in_=x1s[gr % 4][:, 0:512])
                vector.bn_stats(out=bnp[:, 6:12], in_=x1s[gr % 4][:, 512:1024])
                vector.bn_aggr(out=bno2[:, 2 * tg:2 * tg + 2], in_=bnp[:])
                vector.tensor_scalar_add(out=varc[:], in0=bno2[:, 2 * tg + 1:2 * tg + 2],
                                         scalar1=EPS)
                vector.reciprocal(out=rcc[tg][:], in_=varc[:])
                vector.inc(sv_stat, 1)

        def emit_apply(vector, ip1, tg):
            gr = 2 * ip1 + tg
            vector.wait_ge(ss_r2, gr + 1)
            vector.tensor_scalar(out=biasc[tg][:], in0=bno2[:, 2 * tg:2 * tg + 1],
                                 scalar1=r2c[tg][:],
                                 scalar2=-1.0, op0=OP.mult, op1=OP.mult)
            vector.tensor_scalar(out=xs[gr % 2][:], in0=x1s[gr % 4][:],
                                 scalar1=r2c[tg][:], scalar2=biasc[tg][:],
                                 op0=OP.mult, op1=OP.add)

        def emit_stage_block(vector, ip1, Bl):
            tg, d = Bl // 8, Bl % 8
            gr = 2 * ip1 + tg
            src = xs[gr % 2][:, 128 * d:128 * d + 128]
            sl = [l2t[ip1 % 2][s][d][:, 128 * tg:128 * tg + 128] for s in range(3)]
            _split3(vector, src, sl[0], sl[1], sl[2], rr1[:])
            vector.inc(sv_stg, 1)

        block = es.enter_context(nc.Block())

        @block.sync
        def _(sync):
            sync.dma_start(out=lnw_c[:], in_=dlnw[:]).then_inc(A_cfg, 16)
            sync.dma_start(out=lnb_c[:], in_=dlnb[:]).then_inc(A_cfg, 16)
            sync.dma_start(out=xs[0][:, 0:128], in_=dident[:]).then_inc(A_cfg, 16)
            for gr in range(NG):
                if gr == 0:
                    sync.wait_ge(A_id, 1)
                if gr >= 2:
                    sync.wait_ge(A_ln1s, gr - 1)
                sync.dma_start(out=xs[gr % 2][:], in_=xg[gr, :, :]).then_inc(A_xs[gr % 2], 16)
            if not do_b:
                return
            for t in range(32):
                if t >= 2:
                    sync.wait_ge(B_scan, t - 1)
                else:
                    sync.wait_ge(A_ln1s, 31 + t)
                src = wfcTg[t // 4, :, 1024 * (t % 4):1024 * (t % 4) + 1024]
                sync.dma_start(out=xs[t % 2][:], in_=src).then_inc(B_x1s[t % 2], 16)
            for t in range(32):
                if t >= 2:
                    sync.wait_ge(B_q, t - 1)
                else:
                    sync.wait_ge(B_scan, 31 + t)
                src = wfcTg[t // 4, :, 1024 * (t % 4):1024 * (t % 4) + 1024]
                sync.dma_start(out=xs[t % 2][:], in_=src).then_inc(B_x2s[t % 2], 16)
            if not do_main:
                return
            for gr in range(ng):
                if gr < 2:
                    sync.wait_ge(B_q, 32)
                    sync.wait_ge(B_qp, 32)
                else:
                    sync.wait_ge(sv_stg, 8 * (gr - 1))
                sync.dma_start(out=xs[gr % 2][:], in_=xg[gr, :, :]).then_inc(sm_xs[gr % 2], 16)

        @block.gpsimd
        def _(gpsimd):
            for d in range(8):
                if d >= 2:
                    gpsimd.wait_ge(A_wvsplit, d - 1)
                gpsimd.dma_start(out=x1s[d % 2][:], in_=wvTg[d, :, :]).then_inc(A_wvs[d % 2], 16)
            for d in range(8):
                if d == 0:
                    gpsimd.wait_ge(A_wvsplit, 8)
                elif d >= 2:
                    gpsimd.wait_ge(A_wosplit, d - 1)
                gpsimd.dma_start(out=x1s[d % 2][:], in_=woTg[d, :, :]).then_inc(A_wos[d % 2], 16)
            for b in range(2):
                for s in range(3):
                    k = 3 * b + s
                    if k >= 2:
                        gpsimd.wait_ge(st_attS, k - 1)
                    gpsimd.wait_ge(ss_abT, s + 1)
                    for c in range(8):
                        gpsimd.dma_start(out=rowb[k % 2][0:1, 128 * c:128 * c + 128],
                                         in_=rowsT[2 * c + b:2 * c + b + 1,
                                                   128 * s:128 * s + 128]
                                         ).then_inc(A_rows[k % 2], 16)
            if do_b:
                gpsimd.wait_ge(st_attS, 6)
                for m in range(2):
                    for s in range(3):
                        k = 3 * m + s
                        if k >= 2:
                            gpsimd.wait_ge(st_ubcS, k - 1)
                        gpsimd.wait_ge(ss_abT, 4 + s)
                        for c in range(8):
                            gpsimd.dma_start(out=rowb[k % 2][0:1, 128 * c:128 * c + 128],
                                             in_=rowsT[2 * c + m:2 * c + m + 1,
                                                       384 + 128 * s:384 + 128 * s + 128]
                                             ).then_inc(B_rows[k % 2], 16)
            if do_main:
                for gr in range(ng):
                    gpsimd.wait_ge(sv_y3, gr + 1)
                    gpsimd.dma_start(out=og[gr, :, :], in_=x1s[gr % 4][:]).then_inc(sm_outs[gr % 4], 16)
                for k in range(4):
                    cnt = len([g for g in range(ng) if g % 4 == k])
                    if cnt:
                        gpsimd.wait_ge(sm_outs[k], 16 * cnt)
            else:
                gpsimd.wait_ge(A_attevac, 2)
                gpsimd.dma_start(out=og[0, :, :], in_=ATT0[:]).then_inc(sm_outs[0], 16)
                gpsimd.dma_start(out=og[1, :, :], in_=ATT1[:]).then_inc(sm_outs[1], 16)
                if do_b:
                    gpsimd.wait_ge(B_uvbc, 2)
                    gpsimd.dma_start(out=og[2, :, :], in_=U[:]).then_inc(sm_outs[2], 16)
                    gpsimd.dma_start(out=og[3, :, :], in_=V[:]).then_inc(sm_outs[3], 16)
                for k in range(4 if do_b else 2):
                    gpsimd.wait_ge(sm_outs[k], 16)

        @block.scalar
        def _(scalar):
            scalar = _AutoDrain(scalar)
            scalar.preload_act_table(AF.Sqrt)
            if do_b:
                for gr0 in range(2):
                    scalar.dma_start(out=x1s[2 + gr0 % 2][:],
                                     in_=wpTg[gr0, :, :]).then_inc(B_x1p[gr0 % 2], 16)
            for gr in range(NG):
                scalar.wait_ge(A_rc, gr + 1)
                scalar.activation(out=r2c[0][:], in_=rcc[0][:], func=AF.Sqrt)
                scalar.inc(A_r1, 1)
                scalar.wait_ge(A_bias, gr + 1)
                if gr >= 1:
                    scalar.wait_ge(A_ln1, gr)
                scalar.activation(out=ATT1[:], in_=xs[gr % 2][:], func=AF.Identity,
                                  bias=biasc[0][:], scale=r2c[0][:])
                scalar.inc(A_ln1s, 1)
                if do_b and gr + 2 < NG:
                    scalar.wait_ge(B_scanp, gr + 1)
                    scalar.dma_start(out=x1s[2 + gr % 2][:],
                                     in_=wpTg[gr + 2, :, :]).then_inc(B_x1p[gr % 2], 16)
                if gr % 16 == 15:
                    b = gr // 16
                    accdead = U if b == 0 else ATT0
                    for d_ in range(8):
                        g_ = 8 * b + d_
                        scalar.wait_ge(st_trA, g_ + 1)
                        scalar.activation(out=accdead[:, 128 * d_:128 * d_ + 128],
                                          in_=trp_ps[g_ % 2][:], func=AF.Copy)
                        scalar.inc(ss_trevA, 1)
            if do_b:
                for k in range(32):
                    if k < 2:
                        scalar.wait_ge(B_scanp, 31 + k)
                        scalar.dma_start(out=x1s[2 + k % 2][:],
                                         in_=wpTg[k, :, :]).then_inc(B_x2p[k % 2], 16)
                    scalar.wait_ge(B_x2p[k % 2], 16 * (k // 2 + 1))
                    if k == 0:
                        scalar.wait_ge(B_g3, 1)
                    p2 = qf[k // 4][:, 1024 * (k % 4):1024 * (k % 4) + 1024]
                    scalar.activation(out=qp[k][:], in_=x1s[2 + k % 2][:], func=AF.Sign,
                                      scale=rcg[1][:], bias=mhalf[:])
                    scalar.activation(out=p2, in_=x1s[2 + k % 2][:], func=AF.Sign,
                                      scale=rcg[1][:], bias=phalf[:])
                    scalar.inc(B_sgp, 1)
                    if k + 2 < 32:
                        scalar.wait_ge(B_sgp, k + 1)
                        scalar.dma_start(out=x1s[2 + k % 2][:],
                                         in_=wpTg[k + 2, :, :]).then_inc(B_x2p[k % 2], 16)
            if do_b:
                for s in range(3):
                    scalar.wait_ge(st_abT, s + 1)
                    scalar.activation(out=rowsT[0:16, 128 * s:128 * s + 128],
                                      in_=trp_ps[s % 2][0:16, :], func=AF.Copy)
                    scalar.inc(ss_abT, 1)
                for s in range(3):
                    scalar.wait_ge(st_abT, 4 + s)
                    scalar.activation(out=rowsT[0:16, 384 + 128 * s:384 + 128 * s + 128],
                                      in_=trp_ps[(3 + s) % 2][0:16, :], func=AF.Copy)
                    scalar.inc(ss_abT, 1)
            if do_main:
                for tg in range(2):
                    scalar.wait_ge(sv_stat, tg + 1)
                    scalar.activation(out=r2c[tg][:], in_=rcc[tg][:], func=AF.Sqrt)
                    scalar.inc(ss_r2, 1)
            def emit_group_evac(scalar, ip1, q):
                Bl, s_ = q // 3, q % 3
                tg_, d_ = Bl // 8, Bl % 8
                scalar.wait_ge(st_tr, 48 * ip1 + q + 1)
                dst = l2t[ip1 % 2][s_][d_][:, 128 * tg_:128 * tg_ + 128]
                scalar.activation(out=dst, in_=trp_ps[q % 2][:], func=AF.Copy)
                scalar.inc(ss_trev, 1)

            for i in range((nt + 1) if do_main else 0):
                if i >= 1:
                    ii = i - 1
                    for q in range(4):
                        scalar.wait_ge(st_trS, 4 * ii + q + 1)
                        sk, hh = q // 2, q % 2
                        dst_ = (S1 if sk == 0 else S2)[:, 128 * hh:128 * hh + 128]
                        scalar.activation(out=dst_, in_=trp_ps[q % 2][:], func=AF.Copy)
                        scalar.inc(ss_trS, 1)
                    scalar.wait_ge(sv_rc3, i)
                    scalar.activation(out=r3c[0][:], in_=rc3c[0][:],
                                      func=AF.Sqrt, scale=quarc[:])
                    scalar.activation(out=r3c[1][:], in_=rc3c[1][:],
                                      func=AF.Sqrt, scale=quarc[:])
                    scalar.inc(ss_r3, 1)
                if i <= nt - 1:
                    if i == 0:
                        for q in range(48):
                            emit_group_evac(scalar, 0, q)
                    scalar.preload_act_table(AF.Gelu)
                    for j in range(32):
                        cglob = 32 * i + j
                        scalar.wait_ge(st_fcj, cglob + 1)
                        if cglob >= 1:
                            scalar.wait_ge(sv_gdone, cglob)
                        scalar.activation(out=g[j % 2][:], in_=fc_ps[j % 2][:],
                                          func=AF.Gelu)
                        scalar.inc(ss_gelu, 1)
                        scalar.activation(out=gsq[j % 2][:], in_=g[j % 2][:], func=AF.Square)
                        scalar.inc(ss_gsq, 1)
                        if j == 4 and i + 1 <= nt - 1:
                            scalar.preload_act_table(AF.Sqrt)
                            for tg in range(2):
                                scalar.wait_ge(sv_stat, 2 * (i + 1) + tg + 1)
                                scalar.activation(out=r2c[tg][:], in_=rcc[tg][:],
                                                  func=AF.Sqrt)
                                scalar.inc(ss_r2, 1)
                            scalar.preload_act_table(AF.Gelu)
                        if i + 1 <= nt - 1 and 11 <= j <= 31:
                            emit_group_evac(scalar, i + 1, 2 * (j - 11))
                            emit_group_evac(scalar, i + 1, 2 * (j - 11) + 1)
                    if i + 1 <= nt - 1:
                        for q in range(42, 48):
                            emit_group_evac(scalar, i + 1, q)
                    scalar.preload_act_table(AF.Sqrt)

        @block.tensor
        def _(tensor):
            ps_v = fc_ps[0]
            ps_a = fc_ps[1]
            ps_bc = proj_ps[0]
            for b in range(2):
                tensor.wait_ge(A_accsplit, b + 1)
                for d_ in range(8):
                    g_ = 8 * b + d_
                    if g_ >= 2:
                        tensor.wait_ge(ss_trevA, g_ - 1)
                    last = None
                    for s_ in range(3):
                        lhs = l2t[0][s_][d_ // 2][:, 128 * (d_ % 2):128 * (d_ % 2) + 128]
                        last = tensor.matmul(out=trp_ps[g_ % 2][:], lhsT=lhs,
                                             rhs=identb[:], start=(s_ == 0),
                                             stop=(s_ == 2), skip_group_check=True)
                    last.then_inc(st_trA, 1)
            if do_b:
                tensor.wait_ge(B_g1, 1)
                last = None
                for s in range(3):
                    last = tensor.matmul(out=fc_ps[0][0:1, 0:1], lhsT=ones_col[:],
                                  rhs=raccbp[s], start=(s == 0), stop=(s == 2),
                                  skip_group_check=True)
                last.then_inc(st_g1, 1)
                tensor.wait_ge(B_g2, 1)
                last = None
                for s in range(3):
                    last = tensor.matmul(out=fc_ps[1][:, 0:1], lhsT=ones_row[:],
                                  rhs=gambp[s], start=(s == 0), stop=(s == 2),
                                  skip_group_check=True)
                last.then_inc(st_g2, 1)
            first = True
            for d in range(8):
                tensor.wait_ge(A_wvsplit, d + 1)
                if d == 0:
                    tensor.wait_ge(A_sv, 1)
                    if do_b:
                        tensor.wait_ge(B_g3, 1)
                last = None
                for e in range(8):
                    for sw in range(3):
                        lhs = l2t[d % 2][sw][e // 2][:, 128 * (e % 2):128 * (e % 2) + 128]
                        for sa in range(3):
                            last = tensor.matmul(out=ps_v[:, 2 * e:2 * e + 2], lhsT=lhs,
                                          rhs=svb[sa][:, 2 * d:2 * d + 2],
                                          start=first,
                                          stop=(d == 7 and e == 7 and sw == 2 and sa == 2),
                                          skip_group_check=True)
                            first = False
                last.then_inc(st_wv, 1)
            first = True
            for d in range(8):
                tensor.wait_ge(A_wosplit, d + 1)
                if d == 0:
                    tensor.wait_ge(A_v, 1)
                last = None
                for o in range(8):
                    for sw in range(3):
                        lhs = l2t[d % 2][sw][o // 2][:, 128 * (o % 2):128 * (o % 2) + 128]
                        for sa in range(3):
                            last = tensor.matmul(out=ps_a[:, 2 * o:2 * o + 2], lhsT=lhs,
                                          rhs=vb[sa][:, 2 * d:2 * d + 2],
                                          start=first,
                                          stop=(d == 7 and o == 7 and sw == 2 and sa == 2),
                                          skip_group_check=True)
                            first = False
                last.then_inc(st_wo, 1)
            tensor.wait_ge(A_attnsplit, 1)
            for s in range(3):
                if s == 0:
                    tensor.wait_ge(ss_trevA, 15)
                elif s == 1:
                    tensor.wait_ge(ss_trevA, 16)
                else:
                    tensor.wait_ge(ss_abT, 1)
                tensor.matmul(out=trp_ps[s % 2][0:16, :], lhsT=ab[s][:],
                              rhs=identb[:], start=True, stop=True,
                              skip_group_check=True).then_inc(st_abT, 1)
            for b in range(2):
                for s in range(3):
                    k = 3 * b + s
                    tensor.wait_ge(A_rows[k % 2], 16 * 8 * (k // 2 + 1))
                    if b == 1 and s == 0:
                        tensor.wait_ge(A_attevac, 1)
                    last = None
                    for hh in range(2):
                        last = tensor.matmul(out=ps_bc[:, 512 * hh:512 * hh + 512],
                                      lhsT=ones_row[:],
                                      rhs=rowb[k % 2][0:1, 512 * hh:512 * hh + 512],
                                      start=(s == 0), stop=(s == 2), skip_group_check=True)
                    last.then_inc(st_attS, 1)
            if not do_b:
                return
            tensor.wait_ge(B_g1, 2)
            last = None
            for s in range(3):
                last = tensor.matmul(out=fc_ps[0][0:1, 0:1], lhsT=ones_col[:], rhs=raccb[s][:],
                              start=(s == 0), stop=(s == 2), skip_group_check=True)
            last.then_inc(st_g1, 1)
            tensor.wait_ge(B_g2, 2)
            last = None
            for s in range(3):
                last = tensor.matmul(out=fc_ps[1][:, 0:1], lhsT=ones_row[:], rhs=gamb[s][:],
                              start=(s == 0), stop=(s == 2), skip_group_check=True)
            last.then_inc(st_g2, 1)
            tensor.wait_ge(B_lnw, 1)
            tensor.wait_ge(B_qp, 32)
            first = True
            for dc in range(8):
                for j in range(32):
                    for s in range(3):
                        tensor.matmul(out=fc_ps[0][:, 2 * dc:2 * dc + 1],
                                      lhsT=qp[j][:, 128 * dc:128 * dc + 128],
                                      rhs=lnwb[s][:, j:j + 1],
                                      start=first, stop=False, skip_group_check=True)
                        first = False
                        last = tensor.matmul(out=fc_ps[0][:, 2 * dc + 1:2 * dc + 2],
                                      lhsT=qp[j][:, 128 * dc:128 * dc + 128],
                                      rhs=lnbb[s][:, j:j + 1],
                                      start=False,
                                      stop=(dc == 7 and j == 31 and s == 2),
                                      skip_group_check=True)
            last.then_inc(st_uv, 1)
            tensor.wait_ge(B_uv, 1)
            for s in range(3):
                tensor.wait_ge(ss_abT, 2 + s)
                tensor.matmul(out=trp_ps[(3 + s) % 2][0:16, :], lhsT=uvb[s][:],
                              rhs=identb[:], start=True, stop=True,
                              skip_group_check=True).then_inc(st_abT, 1)
            for m in range(2):
                for s in range(3):
                    k = 3 * m + s
                    tensor.wait_ge(B_rows[k % 2], 16 * 8 * (k // 2 + 1))
                    if m == 0 and s == 0:
                        tensor.wait_ge(A_attevac, 2)
                    if m == 1 and s == 0:
                        tensor.wait_ge(B_uvbc, 1)
                    last = None
                    for hh in range(2):
                        last = tensor.matmul(out=ps_bc[:, 512 * hh:512 * hh + 512],
                                      lhsT=ones_row[:],
                                      rhs=rowb[k % 2][0:1, 512 * hh:512 * hh + 512],
                                      start=(s == 0), stop=(s == 2), skip_group_check=True)
                    last.then_inc(st_ubcS, 1)
            if not do_main:
                return
            def emit_group_T(tensor, ip1, ql):
                Bl, s_ = ql // 3, ql % 3
                tg_, d_ = Bl // 8, Bl % 8
                tensor.wait_ge(sv_stg, 16 * ip1 + Bl + 1)
                if ip1 == 0 and ql < 2:
                    tensor.wait_ge(ss_abT, 5 + ql)
                elif ip1 == 1 and ql < 2:
                    tensor.wait_ge(ss_trev, 47 + ql)
                elif ql < 2:
                    tensor.wait_ge(ss_trS, 4 * (ip1 - 2) + 3 + ql)
                else:
                    tensor.wait_ge(ss_trev, 48 * ip1 + ql - 1)
                lhs = l2t[ip1 % 2][s_][d_][:, 128 * tg_:128 * tg_ + 128]
                tensor.matmul(out=trp_ps[ql % 2][:], lhsT=lhs,
                              rhs=identb[:], start=True, stop=True,
                              skip_group_check=True).then_inc(st_tr, 1)

            for i in range(nt):
                if i == 0:
                    for ql in range(48):
                        emit_group_T(tensor, 0, ql)
                for j in range(32):
                    cglob = 32 * i + j
                    if j == 0:
                        tensor.wait_ge(ss_trev, 48 * (i + 1))
                        if i == 0:
                            tensor.wait_ge(B_q, 32)
                    if cglob >= 2:
                        tensor.wait_ge(ss_gelu, cglob - 1)
                    last = None
                    for s in range(3):
                        for d in range(8):
                            last = tensor.matmul(out=fc_ps[j % 2][:],
                                          lhsT=qf[d][:, 128 * j:128 * j + 128],
                                          rhs=l2t[i % 2][s][d][:],
                                          start=(s == 0 and d == 0),
                                          stop=(s == 2 and d == 7))
                    last.then_inc(st_fcj, 1)
                    if j >= 2:
                        emit_proj(tensor, i, j - 2)
                    if i + 1 <= nt - 1 and 10 <= j <= 31:
                        emit_group_T(tensor, i + 1, 2 * (j - 10))
                        emit_group_T(tensor, i + 1, 2 * (j - 10) + 1)
                emit_proj(tensor, i, 30)
                emit_proj(tensor, i, 31)
                if i + 1 <= nt - 1:
                    for ql in range(44, 48):
                        emit_group_T(tensor, i + 1, ql)
                tensor.wait_ge(sv_Ssplit, i + 1)
                Tl = min(i + 1, nt - 1)
                for q in range(4):
                    if q < 2:
                        tensor.wait_ge(ss_trev, 48 * (Tl + 1) - 1 + q)
                    else:
                        tensor.wait_ge(ss_trS, 4 * i + q - 1)
                    sk, hh = q // 2, q % 2
                    last = None
                    for s_ in range(3):
                        lhs = Ss[3 * sk + s_][:, 128 * hh:128 * hh + 128]
                        last = tensor.matmul(out=trp_ps[q % 2][:], lhsT=lhs,
                                             rhs=identb[:], start=(s_ == 0),
                                             stop=(s_ == 2), skip_group_check=True)
                    last.then_inc(st_trS, 1)

        @block.vector
        def _(vector):
            vector = _AutoDrain(vector)
            vector.memset(ones_col[:], 1.0)
            vector.memset(ones_row[:], 1.0)
            vector.memset(phalf[:], 0.5)
            vector.memset(mhalf[:], -0.5)
            vector.memset(quarc[:], 0.25)
            vector.memset(gps_t[:], 0.0)
            vector.wait_ge(A_cfg, 48)
            vector.tensor_copy(out=identb[:], in_=xs[0][:, 0:128])
            vector.inc(A_id, 1)
            _split3(vector, lnw_c[:], lnwb[0][:], lnwb[1][:], lnwb[2][:], rr1w[:], rr2w[:])
            _split3(vector, lnb_c[:], lnbb[0][:], lnbb[1][:], lnbb[2][:], rr1w[:], rr2w[:])
            vector.inc(B_lnw, 1)
            for gr in range(NG):
                b = gr // 16
                ACC = U if b == 0 else ATT0
                vector.wait_ge(A_xs[gr % 2], 16 * (gr // 2 + 1))
                vector.bn_stats(out=bnp[:, 0:6], in_=xs[gr % 2][:, 0:512])
                vector.bn_stats(out=bnp[:, 6:12], in_=xs[gr % 2][:, 512:1024])
                vector.bn_aggr(out=bno[:], in_=bnp[:])
                vector.tensor_scalar_add(out=varc[:], in0=bno[:, 1:2], scalar1=EPS)
                vector.reciprocal(out=rcc[0][:], in_=varc[:])
                vector.inc(A_rc, 1)
                vector.wait_ge(A_r1, gr + 1)
                vector.tensor_scalar(out=biasc[0][:], in0=bno[:, 0:1], scalar1=r2c[0][:],
                                     scalar2=-1.0, op0=OP.mult, op1=OP.mult)
                vector.inc(A_bias, 1)
                vector.wait_ge(A_ln1s, gr + 1)
                if gr % 16 == 0:
                    vector.tensor_copy(out=ACC[:], in_=ATT1[:])
                else:
                    vector.tensor_add(out=ACC[:], in0=ACC[:], in1=ATT1[:])
                vector.inc(A_ln1, 1)
                if do_b:
                    vector.wait_ge(B_x1p[gr % 2], 16 * (gr // 2 + 1))
                    vector.reduce_sum(out=rtmp[:], in_=x1s[2 + gr % 2][:], axis=AX,
                                      apply_absolute_value=True)
                    if gr == 0:
                        vector.tensor_copy(out=raccp[:], in_=rtmp[:])
                    else:
                        vector.tensor_add(out=raccp[:], in0=raccp[:], in1=rtmp[:])
                    vector.inc(B_scanp, 1)
                if gr % 16 == 15:
                    _split3(vector, ACC[:], l2w[0][0], l2w[0][1], l2w[0][2], V[:])
                    vector.inc(A_accsplit, 1)
                    vector.wait_ge(ss_trevA, 8 * (b + 1))
                    for d in range(8):
                        vector.reduce_sum(out=svc[:, 2 * d + b:2 * d + b + 1],
                                          in_=ACC[:, 128 * d:128 * d + 128], axis=AX)
                    vector.inc(A_accred, 1)
            if do_b:
                _split3(vector, raccp[:], raccbp[0], raccbp[1], raccbp[2],
                        rr1s[:, 0:1], rr2s[:, 0:1])
                vector.inc(B_g1, 1)
                vector.wait_ge(st_g1, 1)
                vector.tensor_copy(out=gamr[:], in_=fc_ps[0][0:1, 0:1])
                vector.tensor_scalar(out=gamr[:], in0=gamr[:], scalar1=1.0 / float(D * H),
                                     scalar2=QEPS, op0=OP.mult, op1=OP.add)
                vector.reciprocal(out=gamr2[:], in_=gamr[:])
                _split3(vector, gamr2[:], gambp[0], gambp[1], gambp[2],
                        rr1g[:], rr2g[:])
                vector.inc(B_g2, 1)
                vector.wait_ge(st_g2, 1)
                vector.tensor_copy(out=rcg[1][:], in_=fc_ps[1][:, 0:1])
                vector.inc(B_g3, 1)
            _split3(vector, svc[:], svb[0][:], svb[1][:], svb[2][:], rr1s[:], rr2s[:])
            vector.inc(A_sv, 1)
            for d in range(8):
                vector.wait_ge(A_wvs[d % 2], 16 * (d // 2 + 1))
                if d >= 2:
                    vector.wait_ge(st_wv, d - 1)
                _split3(vector, x1s[d % 2][:], l2w[d % 2][0], l2w[d % 2][1],
                        l2w[d % 2][2], V[:])
                vector.inc(A_wvsplit, 1)
                if do_b:
                    for tq in (2 * d, 2 * d + 1):
                        vector.wait_ge(B_x1s[tq % 2], 16 * (tq // 2 + 1))
                        vector.reduce_sum(out=rtmp[:], in_=xs[tq % 2][:], axis=AX,
                                          apply_absolute_value=True)
                        if tq == 0:
                            vector.tensor_copy(out=raccf[:], in_=rtmp[:])
                        else:
                            vector.tensor_add(out=raccf[:], in0=raccf[:], in1=rtmp[:])
                        vector.inc(B_scan, 1)
            vector.wait_ge(st_wv, 8)
            vector.tensor_copy(out=vcols[:], in_=fc_ps[0][:, 0:16])
            _split3(vector, vcols[:], vb[0][:], vb[1][:], vb[2][:], rr1s[:], rr2s[:])
            vector.inc(A_v, 1)
            for d in range(8):
                vector.wait_ge(A_wos[d % 2], 16 * (d // 2 + 1))
                if d >= 2:
                    vector.wait_ge(st_wo, d - 1)
                _split3(vector, x1s[d % 2][:], l2w[d % 2][0], l2w[d % 2][1],
                        l2w[d % 2][2], V[:])
                vector.inc(A_wosplit, 1)
                if do_b:
                    for tq in (16 + 2 * d, 16 + 2 * d + 1):
                        vector.wait_ge(B_x1s[tq % 2], 16 * (tq // 2 + 1))
                        vector.reduce_sum(out=rtmp[:], in_=xs[tq % 2][:], axis=AX,
                                          apply_absolute_value=True)
                        vector.tensor_add(out=raccf[:], in0=raccf[:], in1=rtmp[:])
                        vector.inc(B_scan, 1)
            vector.wait_ge(st_wo, 8)
            vector.tensor_copy(out=acols[:], in_=fc_ps[1][:, 0:16])
            _split3(vector, acols[:], ab[0][:], ab[1][:], ab[2][:], rr1s[:], rr2s[:])
            vector.inc(A_attnsplit, 1)
            if do_b:
                for t in range(32):
                    vector.wait_ge(B_sgp, t + 1)
                    p2 = qf[t // 4][:, 1024 * (t % 4):1024 * (t % 4) + 1024]
                    vector.tensor_tensor(out=qp[t][:], in0=qp[t][:], in1=p2, op=OP.add)
                    vector.inc(B_qp, 1)
            for b in range(2):
                vector.wait_ge(st_attS, 3 * (b + 1))
                vector.tensor_copy(out=ATT[b][:], in_=proj_ps[0][:])
                vector.inc(A_attevac, 1)
            if not do_b:
                return
            _split3(vector, raccf[:], raccb[0][:], raccb[1][:], raccb[2][:],
                    rr1s[:, 0:1], rr2s[:, 0:1])
            vector.inc(B_g1, 1)
            vector.wait_ge(st_g1, 2)
            vector.tensor_copy(out=gamr[:], in_=fc_ps[0][0:1, 0:1])
            vector.tensor_scalar(out=gamr[:], in0=gamr[:], scalar1=1.0 / float(D * H),
                                 scalar2=QEPS, op0=OP.mult, op1=OP.add)
            vector.reciprocal(out=gamr2[:], in_=gamr[:])
            _split3(vector, gamr2[:], gamb[0][:], gamb[1][:], gamb[2][:],
                    rr1g[:], rr2g[:])
            vector.inc(B_g2, 1)
            vector.wait_ge(st_g2, 2)
            vector.tensor_copy(out=rcg[0][:], in_=fc_ps[1][:, 0:1])
            vector.inc(B_g3, 1)
            for t in range(32):
                vector.wait_ge(B_x2s[t % 2], 16 * (t // 2 + 1))
                if t == 0:
                    vector.wait_ge(B_g3, 2)
                vector.tensor_scalar(out=U[:], in0=xs[t % 2][:], scalar1=rcg[0][:],
                                     scalar2=0.5, op0=OP.mult, op1=OP.is_gt)
                vector.tensor_scalar(out=V[:], in0=xs[t % 2][:], scalar1=rcg[0][:],
                                     scalar2=-0.5, op0=OP.mult, op1=OP.is_lt)
                dst = qf[t // 4][:, 1024 * (t % 4):1024 * (t % 4) + 1024]
                vector.tensor_tensor(out=dst, in0=U[:], in1=V[:], op=OP.subtract)
                vector.inc(B_q, 1)
            vector.wait_ge(st_uv, 1)
            vector.tensor_scalar_mul(out=uvc[:], in0=fc_ps[0][:, 0:16], scalar1=0.5)
            _split3(vector, uvc[:], uvb[0][:], uvb[1][:], uvb[2][:], rr1s[:], rr2s[:])
            vector.inc(B_uv, 1)
            vector.wait_ge(st_ubcS, 3)
            vector.tensor_copy(out=U[:], in_=proj_ps[0][:])
            vector.inc(B_uvbc, 1)
            vector.wait_ge(st_ubcS, 6)
            vector.tensor_copy(out=V[:], in_=proj_ps[0][:])
            vector.inc(B_uvbc, 1)
            if not do_main:
                return
            emit_x1_stats(vector, 0)
            for tg in range(2):
                emit_apply(vector, 0, tg)
            for Bl in range(16):
                emit_stage_block(vector, 0, Bl)
            for i in range(nt):
                for j in range(32):
                    cglob = 32 * i + j
                    if j == 2 and i >= 1:
                        emit_epilogue(vector, i - 1)
                    vector.wait_ge(ss_gelu, cglob + 1)
                    if j == 0:
                        vector.tensor_copy(out=S1[:], in_=g[j % 2][:])
                    else:
                        vector.tensor_add(out=S1[:], in0=S1[:], in1=g[j % 2][:])
                    vector.wait_ge(ss_gsq, cglob + 1)
                    if j == 0:
                        vector.tensor_copy(out=S2[:], in_=gsq[j % 2][:])
                    else:
                        vector.tensor_add(out=S2[:], in0=S2[:], in1=gsq[j % 2][:])
                    if cglob >= 3:
                        vector.wait_ge(st_projj, cglob - 2)
                    gsl = gps[cglob % 3]
                    vector.tensor_scalar_mul(out=gsl[0][:], in0=g[j % 2][:],
                                             scalar1=lnw_c[:, j:j + 1])
                    vector.scalar_tensor_tensor(out=rrq1[:], in0=g[j % 2][:],
                                                scalar=lnw_c[:, j:j + 1],
                                                in1=gsl[0][:],
                                                op0=OP.mult, op1=OP.subtract)
                    vector.tensor_copy(out=gsl[1][:], in_=rrq1[:])
                    vector.tensor_tensor(out=gsl[2][:], in0=rrq1[:],
                                         in1=gsl[1][:], op=OP.subtract)
                    vector.inc(sv_gdone, 1)
                    if i + 1 <= nt - 1:
                        if j == 2:
                            emit_x1_stats(vector, i + 1, 0)
                        elif j == 4:
                            emit_x1_stats(vector, i + 1, 1)
                        elif j == 5:
                            emit_apply(vector, i + 1, 0)
                        elif j == 6:
                            emit_apply(vector, i + 1, 1)
                        elif 7 <= j <= 22:
                            emit_stage_block(vector, i + 1, j - 7)
                if i >= 1:
                    vector.wait_ge(st_trS, 4 * i)
                _split3(vector, S1[:], Ss[0][:], Ss[1][:], Ss[2][:], rrq1[:])
                _split3(vector, S2[:], Ss[3][:], Ss[4][:], Ss[5][:], rrq1[:])
                vector.inc(sv_Ssplit, 1)
                vector.wait_ge(ss_trS, 4 * (i + 1))
                for tg in range(2):
                    vector.reduce_sum(out=m3c[tg][:],
                                      in_=S1[:, 128 * tg:128 * tg + 128], axis=AX)
                    vector.reduce_sum(out=q3c[:],
                                      in_=S2[:, 128 * tg:128 * tg + 128], axis=AX)
                    vector.tensor_scalar_mul(out=m3c[tg][:], in0=m3c[tg][:], scalar1=INV_H)
                    vector.tensor_mul(out=t3c[:], in0=m3c[tg][:], in1=m3c[tg][:])
                    vector.scalar_tensor_tensor(out=var3c[:], in0=q3c[:], scalar=INV_H,
                                                in1=t3c[:], op0=OP.mult, op1=OP.subtract)
                    vector.tensor_scalar_add(out=var3c[:], in0=var3c[:], scalar1=EPS)
                    vector.reciprocal(out=rc3c[tg][:], in_=var3c[:])
                vector.inc(sv_rc3, 1)
            emit_epilogue(vector, nt - 1)

    return nc


_NC_CACHE = None


def kernel(L0_attn_x, L0_attn_w_qkv, L0_attn_w_o, L0_ffn_W_fc, L0_ffn_ln_w,
           L0_ffn_ln_b, L0_ffn_W_proj):
    global _NC_CACHE
    x = np.ascontiguousarray(np.asarray(L0_attn_x, dtype=np.float32))
    wvT = np.ascontiguousarray(np.asarray(L0_attn_w_qkv, dtype=np.float32)[2048:3072, :].T)
    woT = np.ascontiguousarray(np.asarray(L0_attn_w_o, dtype=np.float32).T)
    wfcT = np.ascontiguousarray(np.asarray(L0_ffn_W_fc, dtype=np.float32).T)
    wpT = np.ascontiguousarray(np.asarray(L0_ffn_W_proj, dtype=np.float32).T)
    lnw = np.ascontiguousarray(np.asarray(L0_ffn_ln_w, dtype=np.float32).reshape(32, 128).T)
    lnb = np.ascontiguousarray(np.asarray(L0_ffn_ln_b, dtype=np.float32).reshape(32, 128).T)
    ident = np.eye(128, dtype=np.float32)

    if _NC_CACHE is None:
        _NC_CACHE = build_nc()
    nc = _NC_CACHE

    in_maps = []
    for c in range(NCORES):
        in_maps.append({
            "x": np.ascontiguousarray(x[2 * c:2 * c + 2].reshape(NT, D)),
            "wvT": wvT, "woT": woT, "wfcT": wfcT, "wpT": wpT,
            "lnw": lnw, "lnb": lnb, "ident": ident,
        })
    res = run_bass_kernel_spmd(nc, in_maps, core_ids=list(range(NCORES)))
    out = np.concatenate([r["out"].reshape(2, 2048, D) for r in res.results], axis=0)
    return out


# version 22 (final)
# speedup vs baseline: 1.0171x; 1.0171x over previous
import numpy as np
import concourse.bass as bass
import concourse.mybir as mybir
from concourse.bass_utils import run_bass_kernel_spmd
from contextlib import ExitStack

F32 = mybir.dt.float32
BF16 = mybir.dt.bfloat16
AX = mybir.AxisListType.X
OP = mybir.AluOpType
AF = mybir.ActivationFunctionType

NCORES = 8
NT = 4096
D = 1024
H = 4096
NG = 32
NTILE = 16
EPS = 1e-5
QEPS = 1e-6
INV_D = 1.0 / 1024.0
INV_H = 1.0 / 4096.0




_DRAIN_OPS = frozenset((
    "tensor_copy", "tensor_tensor", "tensor_add", "tensor_sub", "tensor_mul",
    "tensor_max", "tensor_scalar", "tensor_scalar_mul", "tensor_scalar_add",
    "tensor_scalar_sub", "tensor_single_scalar", "scalar_tensor_tensor",
    "reduce_sum", "reduce_max", "tensor_reduce", "reciprocal", "memset",
    "activation", "bn_stats", "bn_aggr", "select", "square", "sqrt", "copy",
    "add", "mul", "nop",
))


class _AutoDrain:

    def __init__(self, eng):
        self._eng = eng
        self._last = None

    def inc(self, sem, n):
        self._last.then_inc(sem, n)

    def __getattr__(self, name):
        attr = getattr(self._eng, name)
        if name in _DRAIN_OPS:
            def wrapped(*a, **kw):
                self._eng.drain()
                ins = attr(*a, **kw)
                self._last = ins
                return ins
            return wrapped
        return attr


def _split3(eng, src_ap, dst_hi, dst_mid, dst_lo, rr1, rr2=None):
    eng.tensor_copy(out=dst_hi, in_=src_ap)
    eng.tensor_tensor(out=rr1, in0=src_ap, in1=dst_hi, op=OP.subtract)
    eng.tensor_copy(out=dst_mid, in_=rr1)
    eng.tensor_tensor(out=dst_lo, in0=rr1, in1=dst_mid, op=OP.subtract)


def build_nc(nt=NTILE, do_b=True, do_main=True):
    ng = 2 * nt
    nc = bass.Bass("TRN2", target_bir_lowering=False, debug=False)

    dx = nc.dram_tensor("x", [NT, D], F32, kind="ExternalInput").ap()
    dwvT = nc.dram_tensor("wvT", [D, D], F32, kind="ExternalInput").ap()
    dwoT = nc.dram_tensor("woT", [D, D], F32, kind="ExternalInput").ap()
    dwfcT = nc.dram_tensor("wfcT", [D, H], F32, kind="ExternalInput").ap()
    dwpT = nc.dram_tensor("wpT", [H, D], F32, kind="ExternalInput").ap()
    dlnw = nc.dram_tensor("lnw", [128, 32], F32, kind="ExternalInput").ap()
    dlnb = nc.dram_tensor("lnb", [128, 32], F32, kind="ExternalInput").ap()
    dident = nc.dram_tensor("ident", [128, 128], F32, kind="ExternalInput").ap()
    dout = nc.dram_tensor("out", [NT, D], F32, kind="ExternalOutput").ap()

    es = ExitStack()
    sb = lambda n, shp, dt=F32: es.enter_context(nc.sbuf_tensor(n, list(shp), dt))
    ps = lambda n, shp: es.enter_context(nc.psum_tensor(n, list(shp), F32))
    sem = lambda n: es.enter_context(nc.semaphore(n))

    with es:
        qf_t = sb("qf_t", (128, 8 * H), BF16)
        qf = [qf_t[:, d * H:(d + 1) * H] for d in range(8)]
        qp_t = sb("qp_t", (128, 32 * D), BF16)
        qp = [qp_t[:, j * D:(j + 1) * D] for j in range(32)]
        lnw_c = sb("lnw_c", (128, 32))
        lnb_c = sb("lnb_c", (128, 32))
        lnwb = [sb(f"lnwb{s}", (128, 32), BF16) for s in range(3)]
        lnbb = [sb(f"lnbb{s}", (128, 32), BF16) for s in range(3)]
        ones_col = sb("ones_col", (128, 1), BF16)
        ones_row = sb("ones_row", (1, 128), BF16)
        identb = sb("identb", (128, 128), BF16)
        U = sb("U", (128, D))
        V = sb("V", (128, D))
        ATT0 = sb("ATT0", (128, D))
        ATT1 = sb("ATT1", (128, D))
        ATT = [ATT0, ATT1]

        xs = [sb(f"xs{k}", (128, D)) for k in range(2)]
        x1s = [sb(f"x1s{k}", (128, D)) for k in range(4)]
        l2t_t = [sb(f"l2t_t{p}", (128, 3 * 8 * 256), BF16) for p in range(2)]
        l2t = [[[l2t_t[p][:, (s * 8 + d) * 256:(s * 8 + d) * 256 + 256] for d in range(8)]
                for s in range(3)] for p in range(2)]
        l2w = [[l2t_t[p][:, s * 2048:s * 2048 + 1024] for s in range(3)]
               for p in range(2)]
        g = [sb("g0", (128, 256))] * 2
        gsq = [sb("gsq0", (128, 256))] * 2
        gp = g
        gps_t = sb("gps_t", (128, 2304), BF16)
        gps = [[gps_t[:, (k * 3 + s) * 256:(k * 3 + s) * 256 + 256] for s in range(3)]
               for k in range(3)]
        rrq1 = sb("rrq1", (128, 256))
        rrq2 = sb("rrq2", (128, 256))
        rr1 = rrq1[:, 0:128]
        rr2 = rrq1[:, 128:256]
        S1 = sb("S1", (128, 256))
        S2 = sb("S2", (128, 256))
        Ss_t = sb("Ss_t", (128, 1536), BF16)
        Ss = [Ss_t[:, 256 * k:256 * k + 256] for k in range(6)]

        bnp = sb("bnp", (128, 12))
        bno = sb("bno", (128, 2))
        bno2 = sb("bno2", (128, 4))
        cols = sb("cols", (128, 32))
        _ci = iter(range(32))
        col = lambda: cols[:, (lambda k: slice(k, k + 1))(next(_ci))]
        c1c = col(); c2c = col()
        m2c = [col() for k in range(2)]
        t2c = col(); varc = col()
        rcc = [col() for k in range(2)]
        r2c = [col() for k in range(2)]
        biasc = [col() for k in range(2)]
        m3c = [col() for k in range(2)]
        q3c = col(); t3c = col(); var3c = col()
        rc3c = [col() for k in range(2)]
        r3c = [col() for k in range(2)]
        nmrc = [col() for k in range(2)]
        phalf = col(); mhalf = col(); quarc = col()
        sum3 = rrq2[:, 0:128]

        svc = sb("svc", (128, 16))
        svb = [sb(f"svb{s}", (128, 16), BF16) for s in range(3)]
        vcols = svc
        vb = [sb(f"vb{s}", (128, 16), BF16) for s in range(3)]
        acols = svc
        ab = svb
        rowb = [l2t_t[0][0:1, 0:D], l2t_t[0][0:1, D:2 * D]]
        raccf = col(); raccp = col()
        rtmp = col()
        raccb = [svb[s][:, 0:1] for s in range(3)]
        gcol = sb("gcol", (128, 8), BF16)
        rowsT = gps_t
        raccbp = [gcol[:, s:s + 1] for s in range(3)]
        gambp = [gcol[0:1, 4 + s:5 + s] for s in range(3)]
        gamr = rrq2[0:1, 226:227]; gamr2 = rrq2[0:1, 227:228]
        gamb = [svb[s][0:1, 1:2] for s in range(3)]
        rcg = [col() for k in range(2)]
        uvc = svc
        uvb = vb
        rr1s = rrq2[:, 128:144]; rr2s = rrq2[:, 144:160]
        rr1w = rrq2[:, 160:192]; rr2w = rrq2[:, 192:224]
        rr1g = rrq2[0:1, 224:225]; rr2g = rrq2[0:1, 225:226]

        fc_ps = [ps("fc_ps0", (128, 256)), ps("fc_ps1", (128, 256))]
        proj_ps = [ps("proj_ps0", (128, D)), ps("proj_ps1", (128, D))]
        trp_ps = [ps("trp_ps0", (128, 128)), ps("trp_ps1", (128, 128))]

        A_cfg = sem("A_cfg")
        A_xs = [sem("A_xs0"), sem("A_xs1")]
        A_wvs = [sem("A_wvs0"), sem("A_wvs1")]
        A_wos = [sem("A_wos0"), sem("A_wos1")]
        st_trA = sem("st_trA"); ss_trevA = sem("ss_trevA")
        A_rows = [sem("A_rows0"), sem("A_rows1")]
        A_sq = sem("A_sq"); A_rc = sem("A_rc"); A_r1 = sem("A_r1"); A_bias = sem("A_bias")
        A_ln1s = sem("A_ln1s"); A_ln1 = sem("A_ln1"); A_accsplit = sem("A_accsplit")
        A_accred = sem("A_accred"); A_id = sem("A_id")
        A_sv = sem("A_sv"); A_wvsplit = sem("A_wvsplit"); A_wosplit = sem("A_wosplit")
        st_wv = sem("st_wv"); st_wo = sem("st_wo"); A_v = sem("A_v")
        A_attnsplit = sem("A_attnsplit"); st_attS = sem("st_attS")
        A_attevac = sem("A_attevac")
        B_x1s = [sem("B_x1s0"), sem("B_x1s1")]
        B_x2s = [sem("B_x2s0"), sem("B_x2s1")]
        B_scan = sem("B_scan")
        B_x1p = [sem("B_x1p0"), sem("B_x1p1")]
        B_x2p = [sem("B_x2p0"), sem("B_x2p1")]
        B_scanp = sem("B_scanp"); B_qp = sem("B_qp")
        B_sgf = sem("B_sgf"); B_sgp = sem("B_sgp")
        st_abT = sem("st_abT"); ss_abT = sem("ss_abT")
        sv_psr = sem("sv_psr"); st_p31a = sem("st_p31a")
        B_g1 = sem("B_g1"); st_g1 = sem("st_g1"); B_g2 = sem("B_g2")
        st_g2 = sem("st_g2"); B_g3 = sem("B_g3"); B_q = sem("B_q")
        B_lnw = sem("B_lnw"); st_uv = sem("st_uv"); B_uv = sem("B_uv")
        B_rows = [sem("B_rows0"), sem("B_rows1")]; st_ubcS = sem("st_ubcS")
        B_uvbc = sem("B_uvbc")
        sm_xs = [sem("sm_xs0"), sem("sm_xs1")]
        st_tr = sem("st_tr"); ss_trev = sem("ss_trev")
        st_trS = sem("st_trS"); ss_trS = sem("ss_trS")
        sm_outs = [sem(f"sm_outs{k}") for k in range(4)]
        sv_x1 = sem("sv_x1"); sv_stat = sem("sv_stat"); sv_bias = sem("sv_bias")
        sv_stg = sem("sv_stg"); sv_gdone = sem("sv_gdone"); sv_yraw = sem("sv_yraw")
        sv_Ssplit = sem("sv_Ssplit"); sv_rc3 = sem("sv_rc3"); sv_y3 = sem("sv_y3")
        ss_sq = sem("ss_sq"); ss_r2 = sem("ss_r2"); ss_ln2 = sem("ss_ln2")
        ss_r3 = sem("ss_r3"); ss_gelu = sem("ss_gelu"); ss_gsq = sem("ss_gsq")
        st_fcj = sem("st_fcj"); st_projj = sem("st_projj")

        xg = dx.rearrange("(g p) d -> g p d", p=128)
        og = dout.rearrange("(g p) d -> g p d", p=128)
        wfcTg = dwfcT.rearrange("(t p) h -> t p h", p=128)
        wpTg = dwpT.rearrange("(t p) d -> t p d", p=128)
        wvTg = dwvT.rearrange("(t p) e -> t p e", p=128)
        woTg = dwoT.rearrange("(t p) o -> t p o", p=128)

        def emit_proj(tensor, i, jj):
            cglob = 32 * i + jj
            tensor.wait_ge(sv_gdone, cglob + 1)
            if jj == 0 and i == 0:
                tensor.wait_ge(B_qp, 32)
                tensor.wait_ge(B_uvbc, 2)
            last = None
            for tg in range(2):
                if jj == 0 and i >= 1:
                    tensor.wait_ge(sv_psr, 2 * i - 1 + tg)
                for s in range(3):
                    for hh in range(2):
                        last = tensor.matmul(out=proj_ps[tg][:, 512 * hh:512 * hh + 512],
                                      lhsT=gps[cglob % 3][s][:, 128 * tg:128 * tg + 128],
                                      rhs=qp[jj][:, 512 * hh:512 * hh + 512],
                                      start=(jj == 0 and s == 0),
                                      stop=(jj == 31 and s == 2),
                                      skip_group_check=True)
                if jj == 31 and tg == 0:
                    last.then_inc(st_p31a, 1)
            last.then_inc(st_projj, 1)

        def emit_epilogue(vector, ii):
            vector.wait_ge(ss_r3, ii + 1)
            for tg in range(2):
                vector.tensor_scalar(out=nmrc[tg][:], in0=m3c[tg][:], scalar1=r3c[tg][:],
                                     scalar2=-2.0, op0=OP.mult, op1=OP.mult)
            for tg in range(2):
                gr = 2 * ii + tg
                if tg == 0:
                    vector.wait_ge(st_p31a, ii + 1)
                else:
                    vector.wait_ge(st_projj, 32 * (ii + 1))
                vector.scalar_tensor_tensor(out=x1s[gr % 4][:], in0=proj_ps[tg][:],
                                            scalar=r3c[tg][:], in1=x1s[gr % 4][:],
                                            op0=OP.mult, op1=OP.add)
                vector.inc(sv_psr, 1)
            for tg in range(2):
                gr = 2 * ii + tg
                vector.scalar_tensor_tensor(out=x1s[gr % 4][:], in0=U[:],
                                            scalar=nmrc[tg][:], in1=x1s[gr % 4][:],
                                            op0=OP.mult, op1=OP.add)
                vector.tensor_add(out=x1s[gr % 4][:], in0=x1s[gr % 4][:], in1=V[:])
                vector.inc(sv_y3, 1)

        def emit_x1_stats(vector, ip1, only_tg=None):
            for tg in ((0, 1) if only_tg is None else (only_tg,)):
                gr = 2 * ip1 + tg
                vector.wait_ge(sm_xs[gr % 2], 16 * (gr // 2 + 1))
                if gr >= 4:
                    vector.wait_ge(sm_outs[gr % 4], 16 * (gr // 4))
                vector.tensor_add(out=x1s[gr % 4][:], in0=xs[gr % 2][:],
                                  in1=ATT[gr // 16][:])
                vector.bn_stats(out=bnp[:, 0:6], in_=x1s[gr % 4][:, 0:512])
                vector.bn_stats(out=bnp[:, 6:12], in_=x1s[gr % 4][:, 512:1024])
                vector.bn_aggr(out=bno2[:, 2 * tg:2 * tg + 2], in_=bnp[:])
                vector.tensor_scalar_add(out=varc[:], in0=bno2[:, 2 * tg + 1:2 * tg + 2],
                                         scalar1=EPS)
                vector.reciprocal(out=rcc[tg][:], in_=varc[:])
                vector.inc(sv_stat, 1)

        def emit_apply(vector, ip1, tg):
            gr = 2 * ip1 + tg
            vector.wait_ge(ss_r2, gr + 1)
            vector.tensor_scalar(out=biasc[tg][:], in0=bno2[:, 2 * tg:2 * tg + 1],
                                 scalar1=r2c[tg][:],
                                 scalar2=-1.0, op0=OP.mult, op1=OP.mult)
            vector.tensor_scalar(out=xs[gr % 2][:], in0=x1s[gr % 4][:],
                                 scalar1=r2c[tg][:], scalar2=biasc[tg][:],
                                 op0=OP.mult, op1=OP.add)

        def emit_stage_block(vector, ip1, Bl):
            tg, d = Bl // 8, Bl % 8
            gr = 2 * ip1 + tg
            src = xs[gr % 2][:, 128 * d:128 * d + 128]
            sl = [l2t[ip1 % 2][s][d][:, 128 * tg:128 * tg + 128] for s in range(3)]
            _split3(vector, src, sl[0], sl[1], sl[2], rr1[:])
            vector.inc(sv_stg, 1)

        block = es.enter_context(nc.Block())

        @block.sync
        def _(sync):
            sync.dma_start(out=lnw_c[:], in_=dlnw[:]).then_inc(A_cfg, 16)
            sync.dma_start(out=lnb_c[:], in_=dlnb[:]).then_inc(A_cfg, 16)
            sync.dma_start(out=xs[0][:, 0:128], in_=dident[:]).then_inc(A_cfg, 16)
            for gr in range(NG):
                if gr == 0:
                    sync.wait_ge(A_id, 1)
                if gr >= 2:
                    sync.wait_ge(A_ln1s, gr - 1)
                sync.dma_start(out=xs[gr % 2][:], in_=xg[gr, :, :]).then_inc(A_xs[gr % 2], 16)
            if not do_b:
                return
            for t in range(32):
                if t >= 2:
                    sync.wait_ge(B_scan, t - 1)
                else:
                    sync.wait_ge(A_ln1s, 31 + t)
                src = wfcTg[t // 4, :, 1024 * (t % 4):1024 * (t % 4) + 1024]
                sync.dma_start(out=xs[t % 2][:], in_=src).then_inc(B_x1s[t % 2], 16)
            for t in range(32):
                if t >= 2:
                    sync.wait_ge(B_q, t - 1)
                else:
                    sync.wait_ge(B_scan, 31 + t)
                src = wfcTg[t // 4, :, 1024 * (t % 4):1024 * (t % 4) + 1024]
                sync.dma_start(out=xs[t % 2][:], in_=src).then_inc(B_x2s[t % 2], 16)
            if not do_main:
                return
            for gr in range(ng):
                if gr < 2:
                    sync.wait_ge(B_q, 32)
                    sync.wait_ge(B_qp, 32)
                else:
                    sync.wait_ge(sv_stg, 8 * (gr - 1))
                sync.dma_start(out=xs[gr % 2][:], in_=xg[gr, :, :]).then_inc(sm_xs[gr % 2], 16)

        @block.gpsimd
        def _(gpsimd):
            for d in range(8):
                if d >= 2:
                    gpsimd.wait_ge(A_wvsplit, d - 1)
                gpsimd.dma_start(out=x1s[d % 2][:], in_=wvTg[d, :, :]).then_inc(A_wvs[d % 2], 16)
            for d in range(8):
                if d == 0:
                    gpsimd.wait_ge(A_wvsplit, 8)
                elif d >= 2:
                    gpsimd.wait_ge(A_wosplit, d - 1)
                gpsimd.dma_start(out=x1s[d % 2][:], in_=woTg[d, :, :]).then_inc(A_wos[d % 2], 16)
            for b in range(2):
                for s in range(3):
                    k = 3 * b + s
                    if k >= 2:
                        gpsimd.wait_ge(st_attS, k - 1)
                    gpsimd.wait_ge(ss_abT, s + 1)
                    for c in range(8):
                        gpsimd.dma_start(out=rowb[k % 2][0:1, 128 * c:128 * c + 128],
                                         in_=rowsT[2 * c + b:2 * c + b + 1,
                                                   128 * s:128 * s + 128]
                                         ).then_inc(A_rows[k % 2], 16)
            if do_b:
                gpsimd.wait_ge(st_attS, 6)
                for m in range(2):
                    for s in range(3):
                        k = 3 * m + s
                        if k >= 2:
                            gpsimd.wait_ge(st_ubcS, k - 1)
                        gpsimd.wait_ge(ss_abT, 4 + s)
                        for c in range(8):
                            gpsimd.dma_start(out=rowb[k % 2][0:1, 128 * c:128 * c + 128],
                                             in_=rowsT[2 * c + m:2 * c + m + 1,
                                                       384 + 128 * s:384 + 128 * s + 128]
                                             ).then_inc(B_rows[k % 2], 16)
            if do_main:
                for gr in range(ng):
                    gpsimd.wait_ge(sv_y3, gr + 1)
                    gpsimd.dma_start(out=og[gr, :, :], in_=x1s[gr % 4][:]).then_inc(sm_outs[gr % 4], 16)
                for k in range(4):
                    cnt = len([g for g in range(ng) if g % 4 == k])
                    if cnt:
                        gpsimd.wait_ge(sm_outs[k], 16 * cnt)
            else:
                gpsimd.wait_ge(A_attevac, 2)
                gpsimd.dma_start(out=og[0, :, :], in_=ATT0[:]).then_inc(sm_outs[0], 16)
                gpsimd.dma_start(out=og[1, :, :], in_=ATT1[:]).then_inc(sm_outs[1], 16)
                if do_b:
                    gpsimd.wait_ge(B_uvbc, 2)
                    gpsimd.dma_start(out=og[2, :, :], in_=U[:]).then_inc(sm_outs[2], 16)
                    gpsimd.dma_start(out=og[3, :, :], in_=V[:]).then_inc(sm_outs[3], 16)
                for k in range(4 if do_b else 2):
                    gpsimd.wait_ge(sm_outs[k], 16)

        @block.scalar
        def _(scalar):
            scalar = _AutoDrain(scalar)
            scalar.preload_act_table(AF.Sqrt)
            if do_b:
                for gr0 in range(2):
                    scalar.dma_start(out=x1s[2 + gr0 % 2][:],
                                     in_=wpTg[gr0, :, :]).then_inc(B_x1p[gr0 % 2], 16)
            for gr in range(NG):
                scalar.wait_ge(A_rc, gr + 1)
                scalar.activation(out=r2c[0][:], in_=rcc[0][:], func=AF.Sqrt)
                scalar.inc(A_r1, 1)
                scalar.wait_ge(A_bias, gr + 1)
                if gr >= 1:
                    scalar.wait_ge(A_ln1, gr)
                scalar.activation(out=ATT1[:], in_=xs[gr % 2][:], func=AF.Identity,
                                  bias=biasc[0][:], scale=r2c[0][:])
                scalar.inc(A_ln1s, 1)
                if do_b and gr + 2 < NG:
                    scalar.wait_ge(B_scanp, gr + 1)
                    scalar.dma_start(out=x1s[2 + gr % 2][:],
                                     in_=wpTg[gr + 2, :, :]).then_inc(B_x1p[gr % 2], 16)
                if gr % 16 == 15:
                    b = gr // 16
                    accdead = U if b == 0 else ATT0
                    for d_ in range(8):
                        g_ = 8 * b + d_
                        scalar.wait_ge(st_trA, g_ + 1)
                        scalar.activation(out=accdead[:, 128 * d_:128 * d_ + 128],
                                          in_=trp_ps[g_ % 2][:], func=AF.Copy)
                        scalar.inc(ss_trevA, 1)
            if do_b:
                for k in range(32):
                    if k < 2:
                        scalar.wait_ge(B_scanp, 31 + k)
                        scalar.dma_start(out=x1s[2 + k % 2][:],
                                         in_=wpTg[k, :, :]).then_inc(B_x2p[k % 2], 16)
                    scalar.wait_ge(B_x2p[k % 2], 16 * (k // 2 + 1))
                    if k == 0:
                        scalar.wait_ge(B_g3, 1)
                    p2 = qf[k // 4][:, 1024 * (k % 4):1024 * (k % 4) + 1024]
                    scalar.activation(out=qp[k][:], in_=x1s[2 + k % 2][:], func=AF.Sign,
                                      scale=rcg[1][:], bias=mhalf[:])
                    scalar.activation(out=p2, in_=x1s[2 + k % 2][:], func=AF.Sign,
                                      scale=rcg[1][:], bias=phalf[:])
                    scalar.inc(B_sgp, 1)
                    if k + 2 < 32:
                        scalar.wait_ge(B_sgp, k + 1)
                        scalar.dma_start(out=x1s[2 + k % 2][:],
                                         in_=wpTg[k + 2, :, :]).then_inc(B_x2p[k % 2], 16)
            if do_b:
                for s in range(3):
                    scalar.wait_ge(st_abT, s + 1)
                    scalar.activation(out=rowsT[0:16, 128 * s:128 * s + 128],
                                      in_=trp_ps[s % 2][0:16, :], func=AF.Copy)
                    scalar.inc(ss_abT, 1)
                for s in range(3):
                    scalar.wait_ge(st_abT, 4 + s)
                    scalar.activation(out=rowsT[0:16, 384 + 128 * s:384 + 128 * s + 128],
                                      in_=trp_ps[(3 + s) % 2][0:16, :], func=AF.Copy)
                    scalar.inc(ss_abT, 1)
            if do_main:
                for tg in range(2):
                    scalar.wait_ge(sv_stat, tg + 1)
                    scalar.activation(out=r2c[tg][:], in_=rcc[tg][:], func=AF.Sqrt)
                    scalar.inc(ss_r2, 1)
            def emit_group_evac(scalar, ip1, q):
                Bl, s_ = q // 3, q % 3
                tg_, d_ = Bl // 8, Bl % 8
                scalar.wait_ge(st_tr, 48 * ip1 + q + 1)
                dst = l2t[ip1 % 2][s_][d_][:, 128 * tg_:128 * tg_ + 128]
                scalar.activation(out=dst, in_=trp_ps[q % 2][:], func=AF.Copy)
                scalar.inc(ss_trev, 1)

            for i in range((nt + 1) if do_main else 0):
                if i >= 1:
                    ii = i - 1
                    for q in range(4):
                        scalar.wait_ge(st_trS, 4 * ii + q + 1)
                        sk, hh = q // 2, q % 2
                        dst_ = (S1 if sk == 0 else S2)[:, 128 * hh:128 * hh + 128]
                        scalar.activation(out=dst_, in_=trp_ps[q % 2][:], func=AF.Copy)
                        scalar.inc(ss_trS, 1)
                    if i == nt:
                        scalar.wait_ge(sv_rc3, i)
                        scalar.activation(out=r3c[0][:], in_=rc3c[0][:],
                                          func=AF.Sqrt, scale=quarc[:])
                        scalar.activation(out=r3c[1][:], in_=rc3c[1][:],
                                          func=AF.Sqrt, scale=quarc[:])
                        scalar.inc(ss_r3, 1)
                if i <= nt - 1:
                    if i == 0:
                        for q in range(48):
                            emit_group_evac(scalar, 0, q)
                    scalar.preload_act_table(AF.Gelu)
                    for j in range(32):
                        cglob = 32 * i + j
                        scalar.wait_ge(st_fcj, cglob + 1)
                        if cglob >= 1:
                            scalar.wait_ge(sv_gdone, cglob)
                        scalar.activation(out=g[j % 2][:], in_=fc_ps[j % 2][:],
                                          func=AF.Gelu)
                        scalar.inc(ss_gelu, 1)
                        scalar.activation(out=gsq[j % 2][:], in_=g[j % 2][:], func=AF.Square)
                        scalar.inc(ss_gsq, 1)
                        if j == 1 and i >= 1:
                            scalar.preload_act_table(AF.Sqrt)
                            scalar.wait_ge(sv_rc3, i)
                            scalar.activation(out=r3c[0][:], in_=rc3c[0][:],
                                              func=AF.Sqrt, scale=quarc[:])
                            scalar.activation(out=r3c[1][:], in_=rc3c[1][:],
                                              func=AF.Sqrt, scale=quarc[:])
                            scalar.inc(ss_r3, 1)
                            scalar.preload_act_table(AF.Gelu)
                        if j == 4 and i + 1 <= nt - 1:
                            scalar.preload_act_table(AF.Sqrt)
                            for tg in range(2):
                                scalar.wait_ge(sv_stat, 2 * (i + 1) + tg + 1)
                                scalar.activation(out=r2c[tg][:], in_=rcc[tg][:],
                                                  func=AF.Sqrt)
                                scalar.inc(ss_r2, 1)
                            scalar.preload_act_table(AF.Gelu)
                        if i + 1 <= nt - 1 and 11 <= j <= 31:
                            emit_group_evac(scalar, i + 1, 2 * (j - 11))
                            emit_group_evac(scalar, i + 1, 2 * (j - 11) + 1)
                    if i + 1 <= nt - 1:
                        for q in range(42, 48):
                            emit_group_evac(scalar, i + 1, q)
                    scalar.preload_act_table(AF.Sqrt)

        @block.tensor
        def _(tensor):
            ps_v = fc_ps[0]
            ps_a = fc_ps[1]
            ps_bc = proj_ps[0]
            for b in range(2):
                tensor.wait_ge(A_accsplit, b + 1)
                for d_ in range(8):
                    g_ = 8 * b + d_
                    if g_ >= 2:
                        tensor.wait_ge(ss_trevA, g_ - 1)
                    last = None
                    for s_ in range(3):
                        lhs = l2t[0][s_][d_ // 2][:, 128 * (d_ % 2):128 * (d_ % 2) + 128]
                        last = tensor.matmul(out=trp_ps[g_ % 2][:], lhsT=lhs,
                                             rhs=identb[:], start=(s_ == 0),
                                             stop=(s_ == 2), skip_group_check=True)
                    last.then_inc(st_trA, 1)
            if do_b:
                tensor.wait_ge(B_g1, 1)
                last = None
                for s in range(3):
                    last = tensor.matmul(out=fc_ps[0][0:1, 0:1], lhsT=ones_col[:],
                                  rhs=raccbp[s], start=(s == 0), stop=(s == 2),
                                  skip_group_check=True)
                last.then_inc(st_g1, 1)
                tensor.wait_ge(B_g2, 1)
                last = None
                for s in range(3):
                    last = tensor.matmul(out=fc_ps[1][:, 0:1], lhsT=ones_row[:],
                                  rhs=gambp[s], start=(s == 0), stop=(s == 2),
                                  skip_group_check=True)
                last.then_inc(st_g2, 1)
            first = True
            for d in range(8):
                tensor.wait_ge(A_wvsplit, d + 1)
                if d == 0:
                    tensor.wait_ge(A_sv, 1)
                    if do_b:
                        tensor.wait_ge(B_g3, 1)
                last = None
                for e in range(8):
                    for sw in range(3):
                        lhs = l2t[d % 2][sw][e // 2][:, 128 * (e % 2):128 * (e % 2) + 128]
                        for sa in range(3):
                            last = tensor.matmul(out=ps_v[:, 2 * e:2 * e + 2], lhsT=lhs,
                                          rhs=svb[sa][:, 2 * d:2 * d + 2],
                                          start=first,
                                          stop=(d == 7 and e == 7 and sw == 2 and sa == 2),
                                          skip_group_check=True)
                            first = False
                last.then_inc(st_wv, 1)
            first = True
            for d in range(8):
                tensor.wait_ge(A_wosplit, d + 1)
                if d == 0:
                    tensor.wait_ge(A_v, 1)
                last = None
                for o in range(8):
                    for sw in range(3):
                        lhs = l2t[d % 2][sw][o // 2][:, 128 * (o % 2):128 * (o % 2) + 128]
                        for sa in range(3):
                            last = tensor.matmul(out=ps_a[:, 2 * o:2 * o + 2], lhsT=lhs,
                                          rhs=vb[sa][:, 2 * d:2 * d + 2],
                                          start=first,
                                          stop=(d == 7 and o == 7 and sw == 2 and sa == 2),
                                          skip_group_check=True)
                            first = False
                last.then_inc(st_wo, 1)
            tensor.wait_ge(A_attnsplit, 1)
            for s in range(3):
                if s == 0:
                    tensor.wait_ge(ss_trevA, 15)
                elif s == 1:
                    tensor.wait_ge(ss_trevA, 16)
                else:
                    tensor.wait_ge(ss_abT, 1)
                tensor.matmul(out=trp_ps[s % 2][0:16, :], lhsT=ab[s][:],
                              rhs=identb[:], start=True, stop=True,
                              skip_group_check=True).then_inc(st_abT, 1)
            for b in range(2):
                for s in range(3):
                    k = 3 * b + s
                    tensor.wait_ge(A_rows[k % 2], 16 * 8 * (k // 2 + 1))
                    if b == 1 and s == 0:
                        tensor.wait_ge(A_attevac, 1)
                    last = None
                    for hh in range(2):
                        last = tensor.matmul(out=ps_bc[:, 512 * hh:512 * hh + 512],
                                      lhsT=ones_row[:],
                                      rhs=rowb[k % 2][0:1, 512 * hh:512 * hh + 512],
                                      start=(s == 0), stop=(s == 2), skip_group_check=True)
                    last.then_inc(st_attS, 1)
            if not do_b:
                return
            tensor.wait_ge(B_g1, 2)
            last = None
            for s in range(3):
                last = tensor.matmul(out=fc_ps[0][0:1, 0:1], lhsT=ones_col[:], rhs=raccb[s][:],
                              start=(s == 0), stop=(s == 2), skip_group_check=True)
            last.then_inc(st_g1, 1)
            tensor.wait_ge(B_g2, 2)
            last = None
            for s in range(3):
                last = tensor.matmul(out=fc_ps[1][:, 0:1], lhsT=ones_row[:], rhs=gamb[s][:],
                              start=(s == 0), stop=(s == 2), skip_group_check=True)
            last.then_inc(st_g2, 1)
            tensor.wait_ge(B_lnw, 1)
            tensor.wait_ge(B_qp, 32)
            first = True
            for dc in range(8):
                for j in range(32):
                    for s in range(3):
                        tensor.matmul(out=fc_ps[0][:, 2 * dc:2 * dc + 1],
                                      lhsT=qp[j][:, 128 * dc:128 * dc + 128],
                                      rhs=lnwb[s][:, j:j + 1],
                                      start=first, stop=False, skip_group_check=True)
                        first = False
                        last = tensor.matmul(out=fc_ps[0][:, 2 * dc + 1:2 * dc + 2],
                                      lhsT=qp[j][:, 128 * dc:128 * dc + 128],
                                      rhs=lnbb[s][:, j:j + 1],
                                      start=False,
                                      stop=(dc == 7 and j == 31 and s == 2),
                                      skip_group_check=True)
            last.then_inc(st_uv, 1)
            tensor.wait_ge(B_uv, 1)
            for s in range(3):
                tensor.wait_ge(ss_abT, 2 + s)
                tensor.matmul(out=trp_ps[(3 + s) % 2][0:16, :], lhsT=uvb[s][:],
                              rhs=identb[:], start=True, stop=True,
                              skip_group_check=True).then_inc(st_abT, 1)
            for m in range(2):
                for s in range(3):
                    k = 3 * m + s
                    tensor.wait_ge(B_rows[k % 2], 16 * 8 * (k // 2 + 1))
                    if m == 0 and s == 0:
                        tensor.wait_ge(A_attevac, 2)
                    if m == 1 and s == 0:
                        tensor.wait_ge(B_uvbc, 1)
                    last = None
                    for hh in range(2):
                        last = tensor.matmul(out=ps_bc[:, 512 * hh:512 * hh + 512],
                                      lhsT=ones_row[:],
                                      rhs=rowb[k % 2][0:1, 512 * hh:512 * hh + 512],
                                      start=(s == 0), stop=(s == 2), skip_group_check=True)
                    last.then_inc(st_ubcS, 1)
            if not do_main:
                return
            def emit_group_T(tensor, ip1, ql):
                Bl, s_ = ql // 3, ql % 3
                tg_, d_ = Bl // 8, Bl % 8
                tensor.wait_ge(sv_stg, 16 * ip1 + Bl + 1)
                if ip1 == 0 and ql < 2:
                    tensor.wait_ge(ss_abT, 5 + ql)
                elif ip1 == 1 and ql < 2:
                    tensor.wait_ge(ss_trev, 47 + ql)
                elif ql < 2:
                    tensor.wait_ge(ss_trS, 4 * (ip1 - 2) + 3 + ql)
                else:
                    tensor.wait_ge(ss_trev, 48 * ip1 + ql - 1)
                lhs = l2t[ip1 % 2][s_][d_][:, 128 * tg_:128 * tg_ + 128]
                tensor.matmul(out=trp_ps[ql % 2][:], lhsT=lhs,
                              rhs=identb[:], start=True, stop=True,
                              skip_group_check=True).then_inc(st_tr, 1)

            for i in range(nt):
                if i == 0:
                    for ql in range(48):
                        emit_group_T(tensor, 0, ql)
                for j in range(32):
                    cglob = 32 * i + j
                    if j == 0:
                        tensor.wait_ge(ss_trev, 48 * (i + 1))
                        if i == 0:
                            tensor.wait_ge(B_q, 32)
                    if cglob >= 2:
                        tensor.wait_ge(ss_gelu, cglob - 1)
                    last = None
                    for s in range(3):
                        for d in range(8):
                            last = tensor.matmul(out=fc_ps[j % 2][:],
                                          lhsT=qf[d][:, 128 * j:128 * j + 128],
                                          rhs=l2t[i % 2][s][d][:],
                                          start=(s == 0 and d == 0),
                                          stop=(s == 2 and d == 7))
                    last.then_inc(st_fcj, 1)
                    if j >= 2:
                        emit_proj(tensor, i, j - 2)
                    if i + 1 <= nt - 1 and 10 <= j <= 31:
                        emit_group_T(tensor, i + 1, 2 * (j - 10))
                        emit_group_T(tensor, i + 1, 2 * (j - 10) + 1)
                emit_proj(tensor, i, 30)
                emit_proj(tensor, i, 31)
                if i + 1 <= nt - 1:
                    for ql in range(44, 48):
                        emit_group_T(tensor, i + 1, ql)
                tensor.wait_ge(sv_Ssplit, i + 1)
                Tl = min(i + 1, nt - 1)
                for q in range(4):
                    if q < 2:
                        tensor.wait_ge(ss_trev, 48 * (Tl + 1) - 1 + q)
                    else:
                        tensor.wait_ge(ss_trS, 4 * i + q - 1)
                    sk, hh = q // 2, q % 2
                    last = None
                    for s_ in range(3):
                        lhs = Ss[3 * sk + s_][:, 128 * hh:128 * hh + 128]
                        last = tensor.matmul(out=trp_ps[q % 2][:], lhsT=lhs,
                                             rhs=identb[:], start=(s_ == 0),
                                             stop=(s_ == 2), skip_group_check=True)
                    last.then_inc(st_trS, 1)

        @block.vector
        def _(vector):
            vector = _AutoDrain(vector)
            vector.memset(ones_col[:], 1.0)
            vector.memset(ones_row[:], 1.0)
            vector.memset(phalf[:], 0.5)
            vector.memset(mhalf[:], -0.5)
            vector.memset(quarc[:], 0.25)
            vector.memset(gps_t[:], 0.0)
            vector.wait_ge(A_cfg, 48)
            vector.tensor_copy(out=identb[:], in_=xs[0][:, 0:128])
            vector.inc(A_id, 1)
            _split3(vector, lnw_c[:], lnwb[0][:], lnwb[1][:], lnwb[2][:], rr1w[:], rr2w[:])
            _split3(vector, lnb_c[:], lnbb[0][:], lnbb[1][:], lnbb[2][:], rr1w[:], rr2w[:])
            vector.inc(B_lnw, 1)
            for gr in range(NG):
                b = gr // 16
                ACC = U if b == 0 else ATT0
                vector.wait_ge(A_xs[gr % 2], 16 * (gr // 2 + 1))
                vector.bn_stats(out=bnp[:, 0:6], in_=xs[gr % 2][:, 0:512])
                vector.bn_stats(out=bnp[:, 6:12], in_=xs[gr % 2][:, 512:1024])
                vector.bn_aggr(out=bno[:], in_=bnp[:])
                vector.tensor_scalar_add(out=varc[:], in0=bno[:, 1:2], scalar1=EPS)
                vector.reciprocal(out=rcc[0][:], in_=varc[:])
                vector.inc(A_rc, 1)
                vector.wait_ge(A_r1, gr + 1)
                vector.tensor_scalar(out=biasc[0][:], in0=bno[:, 0:1], scalar1=r2c[0][:],
                                     scalar2=-1.0, op0=OP.mult, op1=OP.mult)
                vector.inc(A_bias, 1)
                vector.wait_ge(A_ln1s, gr + 1)
                if gr % 16 == 0:
                    vector.tensor_copy(out=ACC[:], in_=ATT1[:])
                else:
                    vector.tensor_add(out=ACC[:], in0=ACC[:], in1=ATT1[:])
                vector.inc(A_ln1, 1)
                if do_b:
                    vector.wait_ge(B_x1p[gr % 2], 16 * (gr // 2 + 1))
                    vector.reduce_sum(out=rtmp[:], in_=x1s[2 + gr % 2][:], axis=AX,
                                      apply_absolute_value=True)
                    if gr == 0:
                        vector.tensor_copy(out=raccp[:], in_=rtmp[:])
                    else:
                        vector.tensor_add(out=raccp[:], in0=raccp[:], in1=rtmp[:])
                    vector.inc(B_scanp, 1)
                if gr % 16 == 15:
                    _split3(vector, ACC[:], l2w[0][0], l2w[0][1], l2w[0][2], V[:])
                    vector.inc(A_accsplit, 1)
                    vector.wait_ge(ss_trevA, 8 * (b + 1))
                    for d in range(8):
                        vector.reduce_sum(out=svc[:, 2 * d + b:2 * d + b + 1],
                                          in_=ACC[:, 128 * d:128 * d + 128], axis=AX)
                    vector.inc(A_accred, 1)
            if do_b:
                _split3(vector, raccp[:], raccbp[0], raccbp[1], raccbp[2],
                        rr1s[:, 0:1], rr2s[:, 0:1])
                vector.inc(B_g1, 1)
                vector.wait_ge(st_g1, 1)
                vector.tensor_copy(out=gamr[:], in_=fc_ps[0][0:1, 0:1])
                vector.tensor_scalar(out=gamr[:], in0=gamr[:], scalar1=1.0 / float(D * H),
                                     scalar2=QEPS, op0=OP.mult, op1=OP.add)
                vector.reciprocal(out=gamr2[:], in_=gamr[:])
                _split3(vector, gamr2[:], gambp[0], gambp[1], gambp[2],
                        rr1g[:], rr2g[:])
                vector.inc(B_g2, 1)
                vector.wait_ge(st_g2, 1)
                vector.tensor_copy(out=rcg[1][:], in_=fc_ps[1][:, 0:1])
                vector.inc(B_g3, 1)
            _split3(vector, svc[:], svb[0][:], svb[1][:], svb[2][:], rr1s[:], rr2s[:])
            vector.inc(A_sv, 1)
            for d in range(8):
                vector.wait_ge(A_wvs[d % 2], 16 * (d // 2 + 1))
                if d >= 2:
                    vector.wait_ge(st_wv, d - 1)
                _split3(vector, x1s[d % 2][:], l2w[d % 2][0], l2w[d % 2][1],
                        l2w[d % 2][2], V[:])
                vector.inc(A_wvsplit, 1)
                if do_b:
                    for tq in (2 * d, 2 * d + 1):
                        vector.wait_ge(B_x1s[tq % 2], 16 * (tq // 2 + 1))
                        vector.reduce_sum(out=rtmp[:], in_=xs[tq % 2][:], axis=AX,
                                          apply_absolute_value=True)
                        if tq == 0:
                            vector.tensor_copy(out=raccf[:], in_=rtmp[:])
                        else:
                            vector.tensor_add(out=raccf[:], in0=raccf[:], in1=rtmp[:])
                        vector.inc(B_scan, 1)
            vector.wait_ge(st_wv, 8)
            vector.tensor_copy(out=vcols[:], in_=fc_ps[0][:, 0:16])
            _split3(vector, vcols[:], vb[0][:], vb[1][:], vb[2][:], rr1s[:], rr2s[:])
            vector.inc(A_v, 1)
            for d in range(8):
                vector.wait_ge(A_wos[d % 2], 16 * (d // 2 + 1))
                if d >= 2:
                    vector.wait_ge(st_wo, d - 1)
                _split3(vector, x1s[d % 2][:], l2w[d % 2][0], l2w[d % 2][1],
                        l2w[d % 2][2], V[:])
                vector.inc(A_wosplit, 1)
                if do_b:
                    for tq in (16 + 2 * d, 16 + 2 * d + 1):
                        vector.wait_ge(B_x1s[tq % 2], 16 * (tq // 2 + 1))
                        vector.reduce_sum(out=rtmp[:], in_=xs[tq % 2][:], axis=AX,
                                          apply_absolute_value=True)
                        vector.tensor_add(out=raccf[:], in0=raccf[:], in1=rtmp[:])
                        vector.inc(B_scan, 1)
            vector.wait_ge(st_wo, 8)
            vector.tensor_copy(out=acols[:], in_=fc_ps[1][:, 0:16])
            _split3(vector, acols[:], ab[0][:], ab[1][:], ab[2][:], rr1s[:], rr2s[:])
            vector.inc(A_attnsplit, 1)
            if do_b:
                for t in range(32):
                    vector.wait_ge(B_sgp, t + 1)
                    p2 = qf[t // 4][:, 1024 * (t % 4):1024 * (t % 4) + 1024]
                    vector.tensor_tensor(out=qp[t][:], in0=qp[t][:], in1=p2, op=OP.add)
                    vector.inc(B_qp, 1)
            for b in range(2):
                vector.wait_ge(st_attS, 3 * (b + 1))
                vector.tensor_copy(out=ATT[b][:], in_=proj_ps[0][:])
                vector.inc(A_attevac, 1)
            if not do_b:
                return
            _split3(vector, raccf[:], raccb[0][:], raccb[1][:], raccb[2][:],
                    rr1s[:, 0:1], rr2s[:, 0:1])
            vector.inc(B_g1, 1)
            vector.wait_ge(st_g1, 2)
            vector.tensor_copy(out=gamr[:], in_=fc_ps[0][0:1, 0:1])
            vector.tensor_scalar(out=gamr[:], in0=gamr[:], scalar1=1.0 / float(D * H),
                                 scalar2=QEPS, op0=OP.mult, op1=OP.add)
            vector.reciprocal(out=gamr2[:], in_=gamr[:])
            _split3(vector, gamr2[:], gamb[0][:], gamb[1][:], gamb[2][:],
                    rr1g[:], rr2g[:])
            vector.inc(B_g2, 1)
            vector.wait_ge(st_g2, 2)
            vector.tensor_copy(out=rcg[0][:], in_=fc_ps[1][:, 0:1])
            vector.inc(B_g3, 1)
            for t in range(32):
                vector.wait_ge(B_x2s[t % 2], 16 * (t // 2 + 1))
                if t == 0:
                    vector.wait_ge(B_g3, 2)
                vector.tensor_scalar(out=U[:], in0=xs[t % 2][:], scalar1=rcg[0][:],
                                     scalar2=0.5, op0=OP.mult, op1=OP.is_gt)
                vector.tensor_scalar(out=V[:], in0=xs[t % 2][:], scalar1=rcg[0][:],
                                     scalar2=-0.5, op0=OP.mult, op1=OP.is_lt)
                dst = qf[t // 4][:, 1024 * (t % 4):1024 * (t % 4) + 1024]
                vector.tensor_tensor(out=dst, in0=U[:], in1=V[:], op=OP.subtract)
                vector.inc(B_q, 1)
            vector.wait_ge(st_uv, 1)
            vector.tensor_scalar_mul(out=uvc[:], in0=fc_ps[0][:, 0:16], scalar1=0.5)
            _split3(vector, uvc[:], uvb[0][:], uvb[1][:], uvb[2][:], rr1s[:], rr2s[:])
            vector.inc(B_uv, 1)
            vector.wait_ge(st_ubcS, 3)
            vector.tensor_copy(out=U[:], in_=proj_ps[0][:])
            vector.inc(B_uvbc, 1)
            vector.wait_ge(st_ubcS, 6)
            vector.tensor_copy(out=V[:], in_=proj_ps[0][:])
            vector.inc(B_uvbc, 1)
            if not do_main:
                return
            emit_x1_stats(vector, 0)
            for tg in range(2):
                emit_apply(vector, 0, tg)
            for Bl in range(16):
                emit_stage_block(vector, 0, Bl)
            for i in range(nt):
                for j in range(32):
                    cglob = 32 * i + j
                    if j == 2 and i >= 1:
                        emit_epilogue(vector, i - 1)
                    vector.wait_ge(ss_gelu, cglob + 1)
                    if j == 0:
                        vector.tensor_copy(out=S1[:], in_=g[j % 2][:])
                    else:
                        vector.tensor_add(out=S1[:], in0=S1[:], in1=g[j % 2][:])
                    vector.wait_ge(ss_gsq, cglob + 1)
                    if j == 0:
                        vector.tensor_copy(out=S2[:], in_=gsq[j % 2][:])
                    else:
                        vector.tensor_add(out=S2[:], in0=S2[:], in1=gsq[j % 2][:])
                    if cglob >= 3:
                        vector.wait_ge(st_projj, cglob - 2)
                    gsl = gps[cglob % 3]
                    vector.tensor_scalar_mul(out=gsl[0][:], in0=g[j % 2][:],
                                             scalar1=lnw_c[:, j:j + 1])
                    vector.scalar_tensor_tensor(out=rrq1[:], in0=g[j % 2][:],
                                                scalar=lnw_c[:, j:j + 1],
                                                in1=gsl[0][:],
                                                op0=OP.mult, op1=OP.subtract)
                    vector.tensor_copy(out=gsl[1][:], in_=rrq1[:])
                    vector.tensor_tensor(out=gsl[2][:], in0=rrq1[:],
                                         in1=gsl[1][:], op=OP.subtract)
                    vector.inc(sv_gdone, 1)
                    if i + 1 <= nt - 1:
                        if j == 2:
                            emit_x1_stats(vector, i + 1, 0)
                        elif j == 4:
                            emit_x1_stats(vector, i + 1, 1)
                        elif j == 5:
                            emit_apply(vector, i + 1, 0)
                        elif j == 6:
                            emit_apply(vector, i + 1, 1)
                        elif 7 <= j <= 22:
                            emit_stage_block(vector, i + 1, j - 7)
                if i >= 1:
                    vector.wait_ge(st_trS, 4 * i)
                _split3(vector, S1[:], Ss[0][:], Ss[1][:], Ss[2][:], rrq1[:])
                _split3(vector, S2[:], Ss[3][:], Ss[4][:], Ss[5][:], rrq1[:])
                vector.inc(sv_Ssplit, 1)
                vector.wait_ge(ss_trS, 4 * (i + 1))
                for tg in range(2):
                    vector.reduce_sum(out=m3c[tg][:],
                                      in_=S1[:, 128 * tg:128 * tg + 128], axis=AX)
                    vector.reduce_sum(out=q3c[:],
                                      in_=S2[:, 128 * tg:128 * tg + 128], axis=AX)
                    vector.tensor_scalar_mul(out=m3c[tg][:], in0=m3c[tg][:], scalar1=INV_H)
                    vector.tensor_mul(out=t3c[:], in0=m3c[tg][:], in1=m3c[tg][:])
                    vector.scalar_tensor_tensor(out=var3c[:], in0=q3c[:], scalar=INV_H,
                                                in1=t3c[:], op0=OP.mult, op1=OP.subtract)
                    vector.tensor_scalar_add(out=var3c[:], in0=var3c[:], scalar1=EPS)
                    vector.reciprocal(out=rc3c[tg][:], in_=var3c[:])
                vector.inc(sv_rc3, 1)
            emit_epilogue(vector, nt - 1)

    return nc


_NC_CACHE = None


def kernel(L0_attn_x, L0_attn_w_qkv, L0_attn_w_o, L0_ffn_W_fc, L0_ffn_ln_w,
           L0_ffn_ln_b, L0_ffn_W_proj):
    global _NC_CACHE
    x = np.ascontiguousarray(np.asarray(L0_attn_x, dtype=np.float32))
    wvT = np.ascontiguousarray(np.asarray(L0_attn_w_qkv, dtype=np.float32)[2048:3072, :].T)
    woT = np.ascontiguousarray(np.asarray(L0_attn_w_o, dtype=np.float32).T)
    wfcT = np.ascontiguousarray(np.asarray(L0_ffn_W_fc, dtype=np.float32).T)
    wpT = np.ascontiguousarray(np.asarray(L0_ffn_W_proj, dtype=np.float32).T)
    lnw = np.ascontiguousarray(np.asarray(L0_ffn_ln_w, dtype=np.float32).reshape(32, 128).T)
    lnb = np.ascontiguousarray(np.asarray(L0_ffn_ln_b, dtype=np.float32).reshape(32, 128).T)
    ident = np.eye(128, dtype=np.float32)

    if _NC_CACHE is None:
        _NC_CACHE = build_nc()
    nc = _NC_CACHE

    in_maps = []
    for c in range(NCORES):
        in_maps.append({
            "x": np.ascontiguousarray(x[2 * c:2 * c + 2].reshape(NT, D)),
            "wvT": wvT, "woT": woT, "wfcT": wfcT, "wpT": wpT,
            "lnw": lnw, "lnb": lnb, "ident": ident,
        })
    res = run_bass_kernel_spmd(nc, in_maps, core_ids=list(range(NCORES)))
    out = np.concatenate([r["out"].reshape(2, 2048, D) for r in res.results], axis=0)
    return out
